# version 39
# baseline (speedup 1.0000x reference)
import os
import numpy as np
import ml_dtypes
import concourse.bass as bass
import concourse.mybir as mybir
from concourse.bass_utils import run_bass_kernel_spmd

F32 = mybir.dt.float32
BF16 = mybir.dt.bfloat16
I32 = mybir.dt.int32
Alu = mybir.AluOpType
Act = mybir.ActivationFunctionType
AxX = mybir.AxisListType.X

NCORES = 8
S_CORE = 1024
NSC = 8
DM = 2048
HC = 16
INTER = 5504
NOC = 43
H = 16
KVH = 4
HD = 128
DQ = H * HD
DKV = KVH * HD
QKV = DQ + 2 * DKV
HSH = DM // NCORES
NICD = 6
EPS_LN = 1e-5
EPS_AT = 1e-6
OBLK = 128
DBG = os.environ.get("MKDBG", "")


class Ctx:
    def __init__(self, nc):
        self.nc = nc
        self.sem = {}
        self.cnt = {}

    def mk_sem(self, stack, *names):
        for n in names:
            self.sem[n] = stack.enter_context(self.nc.semaphore(n))
            self.cnt[n] = 0

    def dma(self, eng, sname, out, in_, **kw):
        eng.dma_start(out=out, in_=in_, **kw).then_inc(self.sem[sname], 16)
        self.cnt[sname] += 16
        return self.cnt[sname]

    def dmat(self, eng, sname, out, in_):
        eng.dma_start_transpose(out, in_).then_inc(self.sem[sname], 16)
        self.cnt[sname] += 16
        return self.cnt[sname]

    def inc(self, ins, sname, n=1):
        ins.then_inc(self.sem[sname], n)
        self.cnt[sname] += n
        return self.cnt[sname]

    def dinc(self, eng, sname, n=1):
        eng.drain().then_inc(self.sem[sname], n)
        self.cnt[sname] += n
        return self.cnt[sname]

    def wait(self, eng, sname, val=None):
        if val is None:
            val = self.cnt[sname]
        if val > 0:
            eng.wait_ge(self.sem[sname], val)


def build_program():
    nc = bass.Bass("TRN2", target_bir_lowering=False)
    import contextlib

    x_in = nc.dram_tensor("x_in", [S_CORE, DM], F32, kind="ExternalInput").ap()
    gidx = nc.dram_tensor("gidx", [INTER, HSH], I32, kind="ExternalInput").ap()
    uidx = nc.dram_tensor("uidx", [INTER, HSH], I32, kind="ExternalInput").ap()
    didx = nc.dram_tensor("didx", [DM, NICD * 128], I32, kind="ExternalInput").ap()
    wq_s = nc.dram_tensor("wq_s", [DQ, HSH], F32, kind="ExternalInput").ap()
    wk_s = nc.dram_tensor("wk_s", [DKV, HSH], F32, kind="ExternalInput").ap()
    wv_s = nc.dram_tensor("wv_s", [DKV, HSH], F32, kind="ExternalInput").ap()
    wo_s = nc.dram_tensor("wo_s", [DM, HSH], F32, kind="ExternalInput").ap()
    lut_in = nc.dram_tensor("lut_in", [16], F32, kind="ExternalInput").ap()
    cos_in = nc.dram_tensor("cos_in", [S_CORE, 64], BF16, kind="ExternalInput").ap()
    sinm_in = nc.dram_tensor("sinm_in", [S_CORE, HD], BF16, kind="ExternalInput").ap()
    maskA_in = nc.dram_tensor("maskA_in", [128, 128], BF16, kind="ExternalInput").ap()
    pmask_in = nc.dram_tensor("pmask_in", [NCORES], F32, kind="ExternalInput").ap()

    out = nc.dram_tensor("out", [S_CORE, DM], F32, kind="ExternalOutput").ap()

    gidx_bf = nc.dram_tensor("gidx_bf", [INTER, HSH], BF16).ap()
    uidx_bf = nc.dram_tensor("uidx_bf", [INTER, HSH], BF16).ap()
    didx_bf = nc.dram_tensor("didx_bf", [DM, NICD * 128], BF16).ap()
    wq_bf = nc.dram_tensor("wq_bf", [DQ, HSH], BF16).ap()
    wk_bf = nc.dram_tensor("wk_bf", [DKV, HSH], BF16).ap()
    wv_bf = nc.dram_tensor("wv_bf", [DKV, HSH], BF16).ap()
    wo_bf = nc.dram_tensor("wo_bf", [DM, HSH], BF16).ap()
    ag_g_in = nc.dram_tensor("ag_g_in", [NOC, 128, 2, 128], BF16).ap()
    CH_B = [0, 11, 22, 33, 43]
    ag_g_out = nc.dram_tensor("ag_g_out", [NCORES, NOC, 128, 2, 128], BF16, addr_space="Shared").ap()
    ag_u_in = nc.dram_tensor("ag_u_in", [NOC, 128, 2, 128], BF16).ap()
    ag_u_out_hi = nc.dram_tensor("ag_u_out_hi", [NCORES, NOC - 22, 128, 2, 128], BF16, addr_space="Shared").ap()
    ag_u_out_lo = nc.dram_tensor("ag_u_out_lo", [NCORES, 22, 128, 2, 128], BF16, addr_space="Shared").ap()
    ag_d_in = nc.dram_tensor("ag_d_in", [NICD, 4, 128, 512], BF16).ap()
    ag_d_out = nc.dram_tensor("ag_d_out", [NCORES, NICD, 4, 128, 512], BF16, addr_space="Shared").ap()
    ag_qkv_in = nc.dram_tensor("ag_qkv_in", [12, 128, 2, 256], BF16).ap()
    ag_qkv_out = nc.dram_tensor("ag_qkv_out", [NCORES, 12, 128, 2, 256], BF16, addr_space="Shared").ap()
    ag_o_in = nc.dram_tensor("ag_o_in", [2, 4, 128, 512], BF16).ap()
    ag_o_out = nc.dram_tensor("ag_o_out", [NCORES, 2, 4, 128, 512], BF16, addr_space="Shared").ap()
    ag_st_in = nc.dram_tensor("ag_st_in", [KVH, 128, 132], BF16).ap()
    ag_st_out = nc.dram_tensor("ag_st_out", [NCORES, KVH, 128, 132], BF16, addr_space="Shared").ap()
    fl_in = nc.dram_tensor("fl_in", [16], BF16).ap()
    fl_outs = {nm: nc.dram_tensor(f"fl_{nm}", [NCORES, 16], BF16).ap() for nm in ("gu", "gu2", "d", "qo", "st")}
    xn_hbm = nc.dram_tensor("xn_hbm", [S_CORE, DM], BF16).ap()
    qk_hbm = nc.dram_tensor("qk_hbm", [S_CORE, DQ + DKV], BF16).ap()

    dbg = {}
    if "x1" in DBG:
        dbg["dbg_x1"] = nc.dram_tensor("dbg_x1", [S_CORE, DM], F32, kind="ExternalOutput").ap()

    if "x2" in DBG:
        dbg["dbg_x2"] = nc.dram_tensor("dbg_x2", [S_CORE, DM], F32, kind="ExternalOutput").ap()
    if "wgb" in DBG:
        dbg["dbg_wgb"] = nc.dram_tensor("dbg_wgb", [NCORES, NOC, 128, 2, 128], F32, kind="ExternalOutput").ap()
    if "wdb" in DBG:
        dbg["dbg_wdb"] = nc.dram_tensor("dbg_wdb", [NCORES, NICD, 4, 128, 512], F32, kind="ExternalOutput").ap()
    if "qk" in DBG:
        dbg["dbg_qk"] = nc.dram_tensor("dbg_qk", [S_CORE, DQ + DKV], F32, kind="ExternalOutput").ap()
    if "vst" in DBG:
        dbg["dbg_vst"] = nc.dram_tensor("dbg_vst", [128, NSC, KVH, 132], F32, kind="ExternalOutput").ap()
    if "stb" in DBG:
        dbg["dbg_stb"] = nc.dram_tensor("dbg_stb", [128, KVH, 8, 132], F32, kind="ExternalOutput").ap()
    if "ofm" in DBG:
        dbg["dbg_ofm"] = nc.dram_tensor("dbg_ofm", [128, 16384], F32, kind="ExternalOutput").ap()

    if "xn1" in DBG:
        dbg["dbg_xn1"] = nc.dram_tensor("dbg_xn1", [128, HC * S_CORE], F32, kind="ExternalOutput").ap()
    if "xs" in DBG:
        dbg["dbg_xs"] = nc.dram_tensor("dbg_xs", [128, NSC, DM], F32, kind="ExternalOutput").ap()
    if "stat" in DBG:
        dbg["dbg_stat"] = nc.dram_tensor("dbg_stat", [128, 8, 8], F32, kind="ExternalOutput").ap()
    if "hfm" in DBG:
        dbg["dbg_hfm"] = nc.dram_tensor("dbg_hfm", [128, NOC * S_CORE], F32, kind="ExternalOutput").ap()

    ctx = Ctx(nc)
    with contextlib.ExitStack() as stack:
        ent = stack.enter_context
        x_sb = ent(nc.sbuf_tensor("x_sb", [128, NSC, DM], F32))
        xnfm = ent(nc.sbuf_tensor("xnfm", [128, HC, S_CORE], BF16))
        h_flat = ent(nc.sbuf_tensor("h_flat", [128, NOC * S_CORE], BF16))
        arena = ent(nc.sbuf_tensor("arena", [128, 8192], BF16))
        tmp_a = ent(nc.sbuf_tensor("tmp_a", [128, 1024], F32))
        tmp_bf = ent(nc.sbuf_tensor("tmp_bf", [128, 2048], BF16))
        gw_i = h_flat[:, 0:INTER]
        gw_t = h_flat[:, INTER : 2 * INTER]
        gw_a = h_flat[:, 2 * INTER : 3 * INTER]
        st_bf = h_flat[:, 20480 : 20480 + 4224].rearrange("p (kv ch j) -> p kv ch j", kv=KVH, ch=8)
        kf_tm = h_flat[:, 24704 : 24704 + 4096].rearrange("p (sc kv d) -> p sc kv d", sc=NSC, kv=KVH)
        v_ext = h_flat[:, 28800 : 28800 + 4224].rearrange("p (sc kv j) -> p sc kv j", sc=NSC, kv=KVH)
        st_all = arena[:, 0 : 2112].rearrange("p (r kv j) -> p r kv j", r=4, kv=KVH)
        st_base = arena[:, 2112 : 2112 + 1056].bitcast(F32).rearrange("p (kv j) -> p kv j", kv=KVH)
        cos_sb = h_flat[:, 33024 : 33024 + 512].rearrange("p (sc d) -> p sc d", sc=NSC)
        sinm_sb = h_flat[:, 33536 : 33536 + 1024].rearrange("p (sc d) -> p sc d", sc=NSC)
        den_sb = h_flat[0:1, 34560 : 34560 + 4096].bitcast(F32).rearrange("p (a b) -> p a b", a=4)
        den_b2 = h_flat[0:1, 38656 : 38656 + 1024].rearrange("p (a b) -> p a b", a=2)
        am_bf = h_flat[:, 39680 : 39680 + 1024].rearrange("p (a b) -> p a b", a=2)
        qs_bf = h_flat[:, 40704 : 40704 + 1024].rearrange("p (a b) -> p a b", a=2)
        am2_bf = h_flat[:, 41728 : 41728 + 1024].rearrange("p (a b) -> p a b", a=2)
        lut_sb = ent(nc.sbuf_tensor("lut_sb", [128, 16], F32))
        maskA = ent(nc.sbuf_tensor("maskA", [128, 128], BF16))
        pmask = ent(nc.sbuf_tensor("pmask", [128, NCORES], F32))
        stat = ent(nc.sbuf_tensor("stat", [128, 8, 16], F32))


        psA = ent(nc.psum_tensor("psA", [128, 2048], F32))
        psB = ent(nc.psum_tensor("psB", [128, 2048], F32))

        ctx.mk_sem(stack, "qs", "qa", "qb", "qc", "qd", "qg", "qx", "qw", "qp", "pl", "wp0", "wp1", "v", "t", "act", "cc")
        C = ctx.cnt

        blk = ent(nc.Block())

        @blk.sync
        def _(sync):
            ctx.dma(sync, "qw", lut_sb[:], lut_in[None, :].broadcast_to([128, 16]))
            ctx.dma(sync, "qw", maskA[:], maskA_in)
            ctx.dma(sync, "qw", pmask[:], pmask_in[None, :].broadcast_to([128, NCORES]))
        c_consts = C["qw"]

        @blk.sync
        def _(sync):
            for sc in range(NSC):
                ctx.dma(sync, "qx", x_sb[:, sc, :],
                        x_in.rearrange("(sc p) d -> p sc d", p=128)[:, sc, :])

        @blk.gpsimd
        def _(g):
            ctx.dma(g, "qw", gidx_bf, gidx)
            ctx.dma(g, "qw", uidx_bf, uidx)
            ctx.dma(g, "qw", didx_bf, didx)
            ctx.dma(g, "qw", wq_bf, wq_s)
            ctx.dma(g, "qw", wk_bf, wk_s)
            ctx.dma(g, "qw", wv_bf, wv_s)
            ctx.dma(g, "qw", wo_bf, wo_s)
        c_x_loaded = C["qx"]
        c_casts = C["qw"]

        @blk.vector
        def _(v):
            v.memset(tmp_a[:], 0.0); ctx.dinc(v, "v")

        def emit_ln_partial(v, dmq):
            v.drain()
            for sc in range(NSC):
                tgt = x_sb[:, sc, dmq * 512 : (dmq + 1) * 512]
                v.reduce_sum(stat[:, sc, 8 + dmq : 9 + dmq], tgt, axis=AxX)
                v.tensor_tensor(tmp_bf[:, 0:512], tgt, tgt, Alu.mult)
                v.reduce_sum(stat[:, sc, 12 + dmq : 13 + dmq], tmp_bf[:, 0:512], axis=AxX)

        def emit_ln_stats(qg_wait=0, partial=False):
            @blk.vector
            def _(v):
                if qg_wait:
                    ctx.wait(v, "qx", qg_wait)
                if partial:
                    v.drain()
                    v.reduce_sum(stat[:, :, 0:1], stat[:, :, 8:12], axis=AxX)
                    v.reduce_sum(stat[:, :, 1:2], stat[:, :, 12:16], axis=AxX)
                else:
                    for sc in range(NSC):
                        v.reduce_sum(stat[:, sc, 0:1], x_sb[:, sc, :], axis=AxX)
                        v.tensor_tensor(tmp_bf[:], x_sb[:, sc, :], x_sb[:, sc, :], Alu.mult)
                        v.reduce_sum(stat[:, sc, 1:2], tmp_bf[:], axis=AxX)
                v.drain()
                v.tensor_scalar(stat[:, :, 2:3], stat[:, :, 0:1], 1.0 / DM, None, Alu.mult)
                v.tensor_scalar(stat[:, :, 3:4], stat[:, :, 1:2], 1.0 / DM, None, Alu.mult)
                v.drain()
                v.tensor_tensor(stat[:, :, 4:5], stat[:, :, 2:3], stat[:, :, 2:3], Alu.mult)
                v.drain()
                v.tensor_tensor(stat[:, :, 5:6], stat[:, :, 3:4], stat[:, :, 4:5], Alu.subtract)
                v.drain()
                ins = v.tensor_scalar(stat[:, :, 5:6], stat[:, :, 5:6], EPS_LN, None, Alu.add)
                ins; ctx.dinc(v, "v")
            c_var = C["v"]

            @blk.scalar
            def _(se):
                ctx.wait(se, "v", c_var)
                se.activation(stat[:, :, 6:7], stat[:, :, 5:6], Act.Sqrt); ctx.dinc(se, "act")
            c_sqrt = C["act"]

            @blk.vector
            def _(v):
                ctx.wait(v, "act", c_sqrt)
                v.reciprocal(stat[:, :, 7:8], stat[:, :, 6:7])
                ctx.dinc(v, "v")

            @blk.vector
            def _(v):
                ctx.wait(v, "v", C["v"])
                v.drain()
                ins = v.scalar_tensor_tensor(stat[:, :, 4:5], stat[:, :, 2:3], -1.0,
                                             stat[:, :, 7:8], Alu.mult, Alu.mult)
                ins; ctx.dinc(v, "v")
            return C["v"]

        def emit_ln_apply(c_nb, deep4=False):
            bufs = [tmp_bf[:], tmp_a[:].bitcast(BF16)]
            snames = ["qa", "qb"]
            if deep4:
                bufs += [h_flat[:, 24704:26752], h_flat[:, 26752:28800]]
                snames += ["qc", "qd"]
            nb_ = len(bufs)
            store_after = {}
            for sc in range(NSC):
                @blk.scalar
                def _(se, sc=sc):
                    sname = snames[sc % nb_]
                    if sc == 0:
                        ctx.wait(se, "v", c_nb)
                    if sc >= nb_:
                        ctx.wait(se, sname, store_after[sc - nb_][1])
                    buf = bufs[sc % nb_]
                    se.activation(buf, x_sb[:, sc, :], Act.Identity,
                                  bias=stat[:, sc, 4:5], scale=stat[:, sc, 7:8])
                    se.drain()
                    store_after[sc] = (sname, ctx.dma(
                        se, sname, xn_hbm.rearrange("(sc p) d -> p sc d", p=128)[:, sc, :], buf))
            return store_after

        def emit_ln_xbars(store_after, xbar_sem="qs", full=False):
            c_t_entry = C["t"]
            if full:
                @blk.sync
                def _(sync):
                    ctx.wait(sync, store_after[NSC - 2][0], store_after[NSC - 2][1])
                    ctx.wait(sync, store_after[NSC - 1][0], store_after[NSC - 1][1])
                    ctx.wait(sync, "t", c_t_entry)
                    for hcx in range(HC):
                        ctx.dmat(sync, xbar_sem, xnfm[:, hcx, :],
                                 xn_hbm[:, hcx * 128 : (hcx + 1) * 128])
                return (xbar_sem, [C[xbar_sem]] * 4)
            pcnt = []
            for p in range(NSC // 2):
                @blk.sync
                def _(sync, p=p):
                    ctx.wait(sync, store_after[2 * p][0], store_after[2 * p][1])
                    ctx.wait(sync, store_after[2 * p + 1][0], store_after[2 * p + 1][1])
                    if p == 0:
                        ctx.wait(sync, "t", c_t_entry)
                    for hcx in range(HC):
                        ctx.dmat(sync, xbar_sem, xnfm[:, hcx, p * 256 : (p + 1) * 256],
                                 xn_hbm[p * 256 : (p + 1) * 256, hcx * 128 : (hcx + 1) * 128])
                pcnt.append(C[xbar_sem])
            return (xbar_sem, pcnt)

        def emit_ln(qg_wait=0, partial=False):
            return emit_ln_xbars(emit_ln_apply(emit_ln_stats(qg_wait, partial=partial), deep4=True))

        cc_after = {}
        c_gu_store = {}
        for j in range(2):
            @blk.sync
            def _(sync, j=j):
                ctx.wait(sync, "qw", c_casts)
                ctx.wait(sync, "v", C["v"])
                ctx.dmat(sync, "qs", gw_i, gidx_bf[:, j * 128 : (j + 1) * 128])

            @blk.vector
            def _(v):
                ctx.wait(v, "qs", C["qs"])
                ctx.wait(v, "wp0", C["wp0"])
                v.tensor_scalar(gw_a, gw_i, 0.0, lut_sb[:, 0:1], Alu.is_equal, Alu.mult)
                for t in range(1, 16):
                    v.tensor_scalar(gw_t, gw_i, float(t), lut_sb[:, t : t + 1], Alu.is_equal, Alu.mult)
                    ins = v.tensor_tensor(gw_a, gw_a, gw_t, Alu.add)
                ins; ctx.dinc(v, "v")

            @blk.scalar
            def _(se, j=j):
                ctx.wait(se, "v", C["v"])
                for c0 in range(0, NOC, 2):
                    cw = min(2, NOC - c0)
                    ctx.dma(se, "wp0", ag_g_in[c0 : c0 + cw, :, j, :].rearrange("oc p o -> p oc o"),
                            gw_a.rearrange("p (oc o) -> p oc o", o=128)[:, c0 : c0 + cw, :])
            c_gu_store[("wp0", j)] = C["wp0"]
        c_gu_gather = C["v"]

        uw_i = h_flat[:, 16512 : 22016]
        uw_t = h_flat[:, 22016 : 27520]
        uw_a = h_flat[:, 27520 : 33024]
        c_upl = {}
        for j in range(2):
            @blk.sync
            def _(sync, j=j):
                ctx.wait(sync, "qw", c_casts)
                if j == 1:
                    ctx.wait(sync, "pl", c_upl[0])
                ctx.dmat(sync, "qs", uw_i, uidx_bf[:, j * 128 : (j + 1) * 128])

            @blk.gpsimd
            def _(g, j=j):
                ctx.wait(g, "qs", C["qs"])
                if j == 1:
                    ctx.wait(g, "wp1", c_gu_store[("wp1", 0)])
                g.tensor_scalar(uw_a, uw_i, 0.0, lut_sb[:, 0:1], Alu.is_equal, Alu.mult)
                for t in range(1, 16):
                    g.tensor_scalar(uw_t, uw_i, float(t), lut_sb[:, t : t + 1], Alu.is_equal, Alu.mult)
                    g.tensor_tensor(uw_a, uw_a, uw_t, Alu.add)
                c_upl[j] = ctx.dinc(g, "pl")

            @blk.scalar
            def _(se, j=j):
                ctx.wait(se, "pl", c_upl[j])
                for c0 in range(0, NOC, 2):
                    cw = min(2, NOC - c0)
                    ctx.dma(se, "wp1", ag_u_in[c0 : c0 + cw, :, j, :].rearrange("oc p o -> p oc o"),
                            uw_a.rearrange("p (oc o) -> p oc o", o=128)[:, c0 : c0 + cw, :])
            c_gu_store[("wp1", j)] = C["wp1"]

        ln1_sa = emit_ln_apply(emit_ln_stats(qg_wait=c_x_loaded))

        RG = [list(range(NCORES))]
        def u_out(r, b):
            return ag_u_out_hi[r, b - 22] if b >= 22 else ag_u_out_lo[r, b]
        def emit_flag(g, nm):
            ctx.wait(g, "cc", C["cc"])
            ins = g.collective_compute("AllGather", Alu.bypass, replica_groups=RG,
                                       ins=[fl_in], outs=[fl_outs[nm]])
            return ctx.inc(ins, "cc")

        @blk.gpsimd
        def _(g):
            ctx.wait(g, "wp0", c_gu_store[("wp0", 1)])
            ins = g.collective_compute("AllGather", Alu.bypass, replica_groups=RG,
                                       ins=[ag_g_in], outs=[ag_g_out])
            ctx.inc(ins, "cc")
            ctx.wait(g, "wp1", c_gu_store[("wp1", 1)])
            ins = g.collective_compute("AllGather", Alu.bypass, replica_groups=RG,
                                       ins=[ag_u_in[22:NOC]], outs=[ag_u_out_hi])
            ctx.inc(ins, "cc")
            c_fl = emit_flag(g, "gu")
            cc_after[("u", 3)] = cc_after[("u", 2)] = c_fl
            ins = g.collective_compute("AllGather", Alu.bypass, replica_groups=RG,
                                       ins=[ag_u_in[0:22]], outs=[ag_u_out_lo])
            ctx.inc(ins, "cc")
            c_fl = emit_flag(g, "gu2")
            cc_after[("u", 1)] = cc_after[("u", 0)] = c_fl
        c_xn1 = emit_ln_xbars(ln1_sa, xbar_sem="qw", full=True)

        c_dg = {}
        c_dst = {}
        def emit_dprep(kd):
            @blk.sync
            def _(sync, kd=kd):
                ctx.wait(sync, "v", c_gu_gather if kd == 0 else c_dg[kd - 1])
                ctx.dmat(sync, "qs", gw_i[:, 0:DM], didx_bf[:, kd * 128 : (kd + 1) * 128])

            @blk.vector
            def _(v, kd=kd):
                ctx.wait(v, "qs", C["qs"])
                if kd == 0:
                    ctx.wait(v, "wp0", c_gu_store[("wp0", 1)])
                else:
                    ctx.wait(v, "qg", c_dst[kd - 1])
                v.tensor_scalar(gw_a[:, 0:DM], gw_i[:, 0:DM], 0.0, lut_sb[:, 0:1], Alu.is_equal, Alu.mult)
                for t in range(1, 16):
                    v.tensor_scalar(gw_t[:, 0:DM], gw_i[:, 0:DM], float(t), lut_sb[:, t : t + 1], Alu.is_equal, Alu.mult)
                    ins = v.tensor_tensor(gw_a[:, 0:DM], gw_a[:, 0:DM], gw_t[:, 0:DM], Alu.add)
                ins; c_dg[kd] = ctx.dinc(v, "v")

            @blk.gpsimd
            def _(g, kd=kd):
                ctx.wait(g, "v", c_dg[kd])
                for q in range(4):
                    ctx.dma(g, "qg", ag_d_in[kd, q][None], gw_a[:, q * 512 : (q + 1) * 512][:, None, :])
            c_dst[kd] = C["qg"]
        D_OF_BLOCK = {42: 0, 40: 1, 38: 2, 36: 3, 34: 4, 32: 5}

        PH = {}
        qkv_pref = {}

        def emit_qkv_prefetch():
            for pos in (0, 1):
                @blk.sync
                def _(sync, pos=pos):
                    if pos == 0:
                        ctx.wait(sync, "cc", cc_after["qkv"])
                        ctx.wait(sync, "t", PH["gate_t"])
                    reg = arena[:, (pos % 2) * 4096 : (pos % 2) * 4096 + 4096].rearrange("p (hc w) -> p hc w", hc=HC)
                    for r in range(NCORES):
                        ctx.dma(sync, "qw", reg[:, 2 * r : 2 * r + 2, :].rearrange("p j o -> p (j o)"),
                                ag_qkv_out[r, 8 + pos].rearrange("p j o -> p (j o)"))
                qkv_pref[pos] = ("qw", C["qw"])

        def emit_attn_prep(c_gate_v, c_gate_t):
            PH["gate_t"] = c_gate_t
            PH["gate_v"] = c_gate_v

            @blk.gpsimd
            def _(g):
                ctx.wait(g, "qg", C["qg"])
                ins = g.collective_compute("AllGather", Alu.bypass, replica_groups=RG,
                                           ins=[ag_d_in], outs=[ag_d_out])
                ctx.inc(ins, "cc")
                cc_after["d"] = emit_flag(g, "d")

        def emit_attn_stage(j):
            qst, wst = 1024, 4096

            @blk.sync
            def _(sync, j=j):
                if j == 0:
                    ctx.wait(sync, "v", PH["gate_v"])
                else:
                    ctx.wait(sync, "qp", C["qp"])
                ctx.dmat(sync, "qp", arena[:, qst : qst + DQ], wq_bf[:, j * 128 : (j + 1) * 128])
                ctx.dmat(sync, "qp", arena[:, qst + DQ : qst + DQ + DKV], wk_bf[:, j * 128 : (j + 1) * 128])
                ctx.dmat(sync, "qp", arena[:, qst + DQ + DKV : qst + QKV], wv_bf[:, j * 128 : (j + 1) * 128])
                ctx.dmat(sync, "qp", arena[:, wst : wst + DM], wo_bf[:, j * 128 : (j + 1) * 128])

            @blk.scalar
            def _(se, j=j):
                ctx.wait(se, "qp", C["qp"])
                for c0 in range(0, 12, 2):
                    ctx.dma(se, "qp", ag_qkv_in[c0 : c0 + 2, :, j, :].rearrange("ct p o -> p ct o"),
                            arena[:, qst : qst + QKV].rearrange("p (ct o) -> p ct o", o=256)[:, c0 : c0 + 2, :])
                for q in range(4):
                    ctx.dma(se, "qp", ag_o_in[j, q][None], arena[:, wst + q * 512 : wst + (q + 1) * 512][:, None, :])

        def emit_qo_ag():
            c_qp = C["qp"]

            @blk.gpsimd
            def _(g):
                ctx.wait(g, "qp", c_qp)
                for ain, aout in [(ag_qkv_in, ag_qkv_out), (ag_o_in, ag_o_out)]:
                    ins = g.collective_compute("AllGather", Alu.bypass, replica_groups=RG,
                                               ins=[ain], outs=[aout])
                    ctx.inc(ins, "cc")
                c_fl = emit_flag(g, "qo")
                cc_after["qkv"] = cc_after["o"] = c_fl

        def ffn1_down_hook(dmq):
            if dmq == 0:
                emit_attn_stage(0)
            elif dmq == 1:
                emit_attn_stage(1)
            elif dmq == 2:
                emit_qo_ag()
            elif dmq == 3:
                emit_qkv_prefetch()


        def emit_ffn(c_xnfm_ready, cc_gu, get_cc_d, final_store=False,
                     block_hook=None, post_gate_hook=None, post_down_hook=None,
                     prefetched=None):
            xn_sem_n, xn_cnts = c_xnfm_ready
            t_entry = C["t"]
            nblk = (INTER + OBLK - 1) // OBLK
            CH_BOUNDS = CH_B

            def slab(base, b, hcx, width):
                off = base + (b % 2) * HC * OBLK + hcx * OBLK
                return arena[:, off : off + width]

            act_free = {0: C["act"], 1: C["act"]}
            v_free = {0: C["v"], 1: C["v"]}
            t_after_grp = {}
            t_after_blk = {}
            c_wg_blk = {}
            kseq = 0
            bs = list(range(nblk - 1, -1, -1))
            for idx, b in enumerate(bs):
                o0 = b * OBLK
                ow = min(OBLK, INTER - o0)
                noc_b = (ow + 127) // 128

                if prefetched is not None and b in prefetched:
                    c_wg_blk[b] = prefetched[b]
                else:
                    @blk.sync
                    def _(sync, b=b, idx=idx, o0=o0, ow=ow):
                        ch = 0 if b < 11 else (1 if b < 22 else (2 if b < 33 else 3))
                        if idx == 0:
                            ctx.wait(sync, "t", t_entry)
                        elif idx >= 2:
                            ctx.wait(sync, "t", t_after_blk[bs[idx - 2]])
                        if b == CH_BOUNDS[ch + 1] - 1:
                            ctx.wait(sync, "cc", cc_gu[ch])
                        wps = "wp0" if b % 2 == 0 else "wp1"
                        for mat, base in ((None, 0), (None, 2 * HC * OBLK)):
                            reg = arena[:, base + (b % 2) * HC * OBLK : base + (b % 2 + 1) * HC * OBLK]
                            regv = reg.rearrange("p (hc w) -> p hc w", hc=HC)
                            for r in range(NCORES):
                                src = ag_g_out[r, b] if base == 0 else u_out(r, b)
                                ctx.dma(sync, wps, regv[:, 2 * r : 2 * r + 2, :].rearrange("p j o -> p (j o)"),
                                        src.rearrange("p j o -> p (j o)"))
                    c_wg_blk[b] = ("wp0" if b % 2 == 0 else "wp1", C["wp0" if b % 2 == 0 else "wp1"])

                @blk.tensor
                def _(te, b=b, ow=ow, noc_b=noc_b, kseq=kseq):
                    te.wait_ge(ctx.sem[c_wg_blk[b][0]], c_wg_blk[b][1])
                    for oj in range(noc_b):
                        k = kseq + oj
                        ocw = min(128, ow - oj * 128)
                        pp = k % 2
                        pg = psA[:, pp * 1024 : pp * 1024 + 1024]
                        pu = psB[:, pp * 1024 : pp * 1024 + 1024]
                        ctx.wait(te, "act", act_free[pp])
                        ctx.wait(te, "v", v_free[pp])
                        for sh in range(2):
                            te.wait_ge(ctx.sem[xn_sem_n], xn_cnts[2 * sh + 1])
                            for hcx in range(HC):
                                te.matmul(pg[:ocw, sh * 512 : (sh + 1) * 512],
                                          slab(0, b, hcx, ow)[:, oj * 128 : oj * 128 + ocw],
                                          xnfm[:, hcx, sh * 512 : (sh + 1) * 512],
                                          start=(hcx == 0), stop=(hcx == HC - 1))
                            for hcx in range(HC):
                                ins = te.matmul(pu[:ocw, sh * 512 : (sh + 1) * 512],
                                          slab(2 * HC * OBLK, b, hcx, ow)[:, oj * 128 : oj * 128 + ocw],
                                          xnfm[:, hcx, sh * 512 : (sh + 1) * 512],
                                          start=(hcx == 0), stop=(hcx == HC - 1))
                        t_after_grp[k] = ctx.inc(ins, "t")
                    t_after_blk[b] = C["t"]

                for oj in range(noc_b):
                    k = kseq + oj
                    oc = (o0 + oj * 128) // 128
                    ocw = min(128, ow - oj * 128)
                    pp = k % 2
                    pg = psA[:, pp * 1024 : pp * 1024 + 1024]
                    pu = psB[:, pp * 1024 : pp * 1024 + 1024]

                    @blk.scalar
                    def _(se, k=k, pg=pg, ocw=ocw, pp=pp):
                        ctx.wait(se, "t", t_after_grp[k])
                        if k >= 2:
                            ctx.wait(se, "v", v_free[pp])
                        ins = se.activation(tmp_bf[:ocw, pp * 1024 : pp * 1024 + 1024], pg[:ocw, :], Act.Silu)
                        ins; act_free[pp] = ctx.dinc(se, "act")

                    @blk.vector
                    def _(v, k=k, pu=pu, oc=oc, ocw=ocw, pp=pp):
                        ctx.wait(v, "act", act_free[pp])
                        ins = v.tensor_tensor(h_flat[:ocw, oc * 1024 : (oc + 1) * 1024], tmp_bf[:ocw, pp * 1024 : pp * 1024 + 1024], pu[:ocw, :], Alu.mult)
                        ins; v_free[pp] = ctx.dinc(v, "v")
                kseq += noc_b
                if block_hook is not None:
                    block_hook(b)
            c_gate_end_t = C["t"]
            c_gate_end_act = C["act"]
            c_gate_end_v = C["v"]
            if post_gate_hook is not None:
                post_gate_hook(c_gate_end_v, c_gate_end_t)

            c_pp1_free = v_free[1]
            wd_t_after = {}
            v_after_dmq = {}
            v_evac = {}
            fs_dma_after = {}
            for dmq in range(4):
                for ic in range(NOC):
                    k = dmq * NOC + ic
                    dtile = tmp_bf[:, 1024 + (k % 2) * 512 : 1024 + (k % 2) * 512 + 512]

                    @blk.sync
                    def _(sync, dmq=dmq, ic=ic, k=k, dtile=dtile):
                        if k == 0:
                            ctx.wait(sync, "cc", get_cc_d())
                            ctx.wait(sync, "v", c_pp1_free)
                        elif k >= 2:
                            ctx.wait(sync, "t", wd_t_after[k - 2])
                        ctx.dma(sync, "wp0" if k % 2 == 0 else "wp1", dtile,
                                ag_d_out[ic % NCORES, ic // NCORES, dmq])
                    c_wd = C["wp0" if k % 2 == 0 else "wp1"]

                    c_v0 = None
                    if ic == 0:
                        c_v0 = C["v"] if dmq == 0 else v_evac[dmq - 1]

                    @blk.tensor
                    def _(te, dmq=dmq, ic=ic, k=k, c_wd=c_wd, c_v0=c_v0, dtile=dtile):
                        ctx.wait(te, "wp0" if k % 2 == 0 else "wp1", c_wd)
                        if c_v0 is not None:
                            ctx.wait(te, "v", c_v0)
                        if ic == 0 and dmq == 0:
                            ctx.wait(te, "act", C["act"])
                        for sc in range(NSC):
                            ps = psA if sc < 4 else psB
                            ins = te.matmul(ps[:, (sc % 4) * 512 : (sc % 4) * 512 + 512],
                                            h_flat[:, ic * 1024 + sc * 128 : ic * 1024 + (sc + 1) * 128],
                                            dtile,
                                            start=(ic == 0), stop=(ic == NOC - 1))
                        wd_t_after[k] = ctx.inc(ins, "t")

                if not final_store:
                    @blk.vector
                    def _(v, dmq=dmq):
                        ctx.wait(v, "t", wd_t_after[dmq * NOC + NOC - 1])
                        for sc in range(NSC):
                            ps = psA if sc < 4 else psB
                            pv = ps[:, (sc % 4) * 512 : (sc % 4) * 512 + 512]
                            tgt = x_sb[:, sc, dmq * 512 : (dmq + 1) * 512]
                            ins = v.scalar_tensor_tensor(tgt, pv, 0.5, tgt, Alu.mult, Alu.add)
                        ins; ctx.dinc(v, "v")
                    v_evac[dmq] = C["v"]

                    @blk.vector
                    def _(v, dmq=dmq):
                        emit_ln_partial(v, dmq)
                        ctx.dinc(v, "v")
                else:
                    @blk.vector
                    def _(v, dmq=dmq):
                        ctx.wait(v, "t", wd_t_after[dmq * NOC + NOC - 1])
                        for sc in range(NSC):
                            ps = psA if sc < 4 else psB
                            pv = ps[:, (sc % 4) * 512 : (sc % 4) * 512 + 512]
                            tgt = x_sb[:, sc, dmq * 512 : (dmq + 1) * 512]
                            ins = v.scalar_tensor_tensor(tgt, pv, 0.5, tgt, Alu.mult, Alu.add)
                        ins; ctx.dinc(v, "v")

                    v_evac[dmq] = C["v"]

                    @blk.scalar
                    def _(se, dmq=dmq):
                        ctx.wait(se, "v", C["v"])
                        fs_dma_after[dmq] = ctx.dma(
                            se, "qg",
                            out.rearrange("(sc p) d -> p sc d", p=128)[:, :, dmq * 512 : (dmq + 1) * 512],
                            x_sb[:, :, dmq * 512 : (dmq + 1) * 512])
                v_after_dmq[dmq] = C["v"]
                if post_down_hook is not None:
                    post_down_hook(dmq)
            return {"v_dmq": v_after_dmq, "gate_t": c_gate_end_t}

        if "wgb" in DBG:
            @blk.gpsimd
            def _(g):
                ctx.wait(g, "cc", cc_after[("u", 0)])
                ctx.dma(g, "qg", dbg["dbg_wgb"], ag_g_out)
        if "wdb" in DBG:
            @blk.gpsimd
            def _(g):
                ctx.wait(g, "cc", cc_after["d"])
                ctx.dma(g, "qg", dbg["dbg_wdb"], ag_d_out)
        if "xs" in DBG:
            @blk.gpsimd
            def _(g):
                ctx.wait(g, "qx", c_x_loaded)
                ctx.dma(g, "qg", dbg["dbg_xs"], x_sb[:])
        if "xn1" in DBG:
            @blk.gpsimd
            def _(g):
                ctx.wait(g, "qw", c_xn1[1][3])
                ctx.dma(g, "qg", dbg["dbg_xn1"], xnfm[:])
        if "stat" in DBG:
            @blk.gpsimd
            def _(g):
                ctx.wait(g, "qw", c_xn1[1][3])
                ctx.dma(g, "qg", dbg["dbg_stat"], stat[:])
        ffn1_info = emit_ffn(c_xn1, [cc_after[("u", i)] for i in range(4)], lambda: cc_after["d"],
                             block_hook=lambda b: emit_dprep(D_OF_BLOCK[b]) if b in D_OF_BLOCK else None,
                             post_gate_hook=emit_attn_prep,
                             post_down_hook=ffn1_down_hook)
        if "hfm" in DBG:
            @blk.gpsimd
            def _(g):
                ctx.wait(g, "v", C["v"])
                ctx.dma(g, "qg", dbg["dbg_hfm"], h_flat[:])

        if "x1" in DBG:
            @blk.gpsimd
            def _(g):
                ctx.wait(g, "v", C["v"])
                for sc in range(NSC):
                    ctx.dma(g, "qg", dbg["dbg_x1"].rearrange("(sc p) d -> p sc d", p=128)[:, sc, :], x_sb[:, sc, :])


        def emit_attention():
            c_xn2 = emit_ln(partial=True)
            KO = 16384

            NCT = 12
            CTW = 256
            t_entry_q = C["t"]
            qkv_t = {}
            v_evac = {}
            act_aft = {}
            store_q = {}
            c_w_ct = {}
            ct_order = [8, 9, 10, 11, 0, 1, 2, 3, 4, 5, 6, 7]
            st_t = {}
            st_v = {}
            for pos, ct in enumerate(ct_order):
                if pos < 2:
                    c_w_ct[pos] = qkv_pref[pos]
                else:
                    @blk.sync
                    def _(sync, ct=ct, pos=pos):
                        ctx.wait(sync, "t", qkv_t[(ct_order[pos - 2], NSC - 1)])
                        reg = arena[:, (pos % 2) * 4096 : (pos % 2) * 4096 + 4096].rearrange("p (hc w) -> p hc w", hc=HC)
                        wps = "wp0" if pos % 2 == 0 else "wp1"
                        for r in range(NCORES):
                            ctx.dma(sync, wps, reg[:, 2 * r : 2 * r + 2, :].rearrange("p j o -> p (j o)"),
                                    ag_qkv_out[r, ct].rearrange("p j o -> p (j o)"))
                    c_w_ct[pos] = ("wp0" if pos % 2 == 0 else "wp1", C["wp0" if pos % 2 == 0 else "wp1"])

                if pos == 0:
                    @blk.gpsimd
                    def _(g):
                        ctx.wait(g, "t", t_entry_q)
                        ctx.dma(g, "qw", cos_sb[:], cos_in.rearrange("(sc p) d -> p sc d", p=128))
                        ctx.dma(g, "qw", sinm_sb[:], sinm_in.rearrange("(sc p) d -> p sc d", p=128))
                    c_cos2 = C["qw"]

                if ct == 10:
                    @blk.vector
                    def _(v):
                        ctx.wait(v, "t", qkv_t[(9, NSC - 1)])
                        v.memset(v_ext[:, :, :, 128:129], 1.0)
                        v.memset(v_ext[:, :, :, 129:132], 0.0); ctx.dinc(v, "v")
                for sc in range(NSC):
                    kq = pos * NSC + sc
                    pp = kq % 2
                    psq = psA[:, pp * 512 : pp * 512 + CTW]

                    @blk.tensor
                    def _(te, ct=ct, sc=sc, pos=pos, kq=kq, pp=pp, psq=psq):
                        te.wait_ge(ctx.sem[c_w_ct[pos][0]], c_w_ct[pos][1])
                        te.wait_ge(ctx.sem[c_xn2[0]], c_xn2[1][sc // 2])
                        if kq == 0:
                            ctx.wait(te, "v", C["v"])
                        if kq >= 2:
                            ctx.wait(te, "v", v_evac[kq - 2])
                        for hcx in range(HC):
                            ins = te.matmul(psq, xnfm[:, hcx, sc * 128 : (sc + 1) * 128],
                                            arena[:, (pos % 2) * 4096 + hcx * CTW : (pos % 2) * 4096 + hcx * CTW + CTW],
                                            start=(hcx == 0), stop=(hcx == HC - 1))
                        qkv_t[(ct, sc)] = ctx.inc(ins, "t")

                    if ct < 10:
                        @blk.vector
                        def _(v, ct=ct, sc=sc, kq=kq, pp=pp, psq=psq):
                            ctx.wait(v, "t", qkv_t[(ct, sc)])
                            if kq == 0:
                                ctx.wait(v, "qw", c_cos2)
                            if kq >= 2:
                                ctx.wait(v, "act", act_aft.get(kq - 2, 0))
                            if ct < 8 and (kq - 1) in store_q:
                                ctx.wait(v, "qa", store_q[kq - 1])
                            psv = psq.rearrange("p (h two d) -> p h two d", h=2, two=2)
                            psw = psv[:, :, ::-1, :]
                            slotA = tmp_a[:, pp * 512 : pp * 512 + CTW]
                            slotB = tmp_a[:, pp * 512 + 256 : pp * 512 + 256 + CTW]
                            cosb = cos_sb[:, sc, None, None, :].broadcast_to([128, 2, 2, 64])
                            sinb = sinm_sb[:, sc, :].rearrange("p (two d) -> p two d", two=2)[
                                :, None, :, :].broadcast_to([128, 2, 2, 64])
                            v.tensor_tensor(slotA, psq, cosb, Alu.mult)
                            v.tensor_tensor(slotB, psw, sinb, Alu.mult)
                            v.drain()
                            v.tensor_tensor(slotA, slotA, slotB, Alu.add)
                            v.drain()
                            ins = v.tensor_single_scalar(slotB, slotA, 0.0, Alu.min)
                            ins; ctx.dinc(v, "v")
                        c_rope = C["v"]

                        @blk.scalar
                        def _(se, kq=kq, pp=pp, c_rope=c_rope):
                            ctx.wait(se, "v", c_rope)
                            se.activation(tmp_bf[:, pp * 1024 + 512 : pp * 1024 + 512 + CTW],
                                          tmp_a[:, pp * 512 + 256 : pp * 512 + 256 + CTW], Act.Exp)
                            act_aft[kq] = ctx.dinc(se, "act")

                        @blk.vector
                        def _(v, ct=ct, sc=sc, kq=kq, pp=pp):
                            ctx.wait(v, "act", act_aft[kq])
                            if ct < 8:
                                dest = tmp_bf[:, pp * 1024 : pp * 1024 + CTW]
                            else:
                                dest = kf_tm[:, sc, 2 * (ct - 8) : 2 * (ct - 8) + 2, :]
                            v.tensor_single_scalar(tmp_a[:, pp * 512 + 256 : pp * 512 + 256 + CTW],
                                                   tmp_a[:, pp * 512 : pp * 512 + CTW], 0.0, Alu.max)
                            v.drain()
                            ins = v.tensor_tensor(dest, tmp_bf[:, pp * 1024 + 512 : pp * 1024 + 512 + CTW],
                                                  tmp_a[:, pp * 512 + 256 : pp * 512 + 256 + CTW], Alu.add)
                            ins; v_evac[kq] = ctx.dinc(v, "v")

                        if ct < 8:
                            @blk.scalar
                            def _(se, ct=ct, sc=sc, kq=kq, pp=pp):
                                ctx.wait(se, "v", v_evac[kq])
                                store_q[kq] = ctx.dma(
                                    se, "qa",
                                    qk_hbm.rearrange("(sc p) c -> p sc c", p=128)[:, sc, ct * CTW : (ct + 1) * CTW],
                                    tmp_bf[:, pp * 1024 : pp * 1024 + CTW])
                    else:
                        @blk.vector
                        def _(v, ct=ct, sc=sc, kq=kq, psq=psq):
                            ctx.wait(v, "t", qkv_t[(ct, sc)])
                            ins = v.tensor_copy(v_ext[:, sc, 2 * (ct - 10) : 2 * (ct - 10) + 2, 0:128],
                                                psq.rearrange("p (kv d) -> p kv d", kv=2))
                            ins; v_evac[kq] = ctx.dinc(v, "v")

                if 4 <= pos <= 7:
                    for chs in (2 * (pos - 4), 2 * (pos - 4) + 1):
                        @blk.tensor
                        def _(te, ch=chs):
                            if ch == 0:
                                ctx.wait(te, "v", C["v"])
                            else:
                                ctx.wait(te, "v", st_v[ch - 1])
                            for kv in range(KVH):
                                ins = te.matmul(psB[:, kv * 512 : kv * 512 + 132],
                                                kf_tm[:, ch, kv, :],
                                                v_ext[:, ch, kv, :],
                                                start=(ch == 0), stop=(ch == NSC - 1))
                            st_t[ch] = ctx.inc(ins, "t")

                        @blk.vector
                        def _(v, ch=chs):
                            ctx.wait(v, "t", st_t[ch])
                            ins = v.tensor_copy(st_bf[:, :, ch, :],
                                                psB[:].rearrange("p (kv r) -> p kv r", kv=4)[:, :, 0:132])
                            ins; st_v[ch] = ctx.dinc(v, "v")
                if pos == 7:
                    @blk.gpsimd
                    def _(g):
                        ctx.wait(g, "v", st_v[NSC - 1])
                        for kv in range(KVH):
                            ctx.dma(g, "qg", ag_st_in[kv][None].rearrange("a p j -> p a j"),
                                    st_bf[:, kv : kv + 1, NSC - 1, :])
                        ctx.wait(g, "qg", C["qg"])
                        ins = g.collective_compute("AllGather", Alu.bypass, replica_groups=RG,
                                                   ins=[ag_st_in], outs=[ag_st_out])
                        ctx.inc(ins, "cc")
                        emit_flag(g, "st")
                    c_cc_st = C["cc"]

            @blk.scalar
            def _(se):
                ctx.wait(se, "v", v_evac[2 * NSC - 1])
                ctx.dma(se, "qa", qk_hbm.rearrange("(sc p) c -> p sc c", p=128)[:, :, DQ : DQ + DKV],
                        kf_tm[:].rearrange("p sc kv d -> p sc (kv d)"))
            c_kf_store = C["qa"]

            @blk.sync
            def _(sync):
                ctx.wait(sync, "qa", max(c_kf_store, max(store_q.values())))
                ctx.wait(sync, "t", qkv_t[(ct_order[NCT - 1], NSC - 1)])
                for hq in range(H):
                    ctx.dmat(sync, "qs", xnfm[:, hq, :], qk_hbm[:, hq * 128 : (hq + 1) * 128])
                for kv in range(KVH):
                    ctx.dmat(sync, "qs", h_flat[:, KO + kv * 1024 : KO + (kv + 1) * 1024],
                             qk_hbm[:, DQ + kv * 128 : DQ + (kv + 1) * 128])
            c_fm = C["qs"]

            if "qk" in DBG:
                @blk.gpsimd
                def _(g):
                    ctx.wait(g, "qs", c_fm)
                    ctx.dma(g, "qg", dbg["dbg_qk"], qk_hbm)
            @blk.sync
            def _(sync):
                ctx.wait(sync, "cc", c_cc_st)
                ctx.wait(sync, "t", qkv_t[(ct_order[10], NSC - 1)])
                for r in range(4):
                    for kv in range(KVH):
                        ctx.dma(sync, "qg", st_all[:, r, kv : kv + 1], ag_st_out[r, kv][None].rearrange("a p j -> p a j"))
            c_stall0 = C["qg"]

            @blk.vector
            def _(v):
                ctx.wait(v, "qg", c_stall0)
                v.memset(st_base[:], 0.0)
                v.drain()
                for r in range(4):
                    v.scalar_tensor_tensor(st_base[:], st_all[:, r], pmask[:, r : r + 1],
                                           st_base[:], Alu.mult, Alu.add)
                    v.drain()
                ctx.dinc(v, "v")
            c_base0 = C["v"]

            @blk.sync
            def _(sync):
                ctx.wait(sync, "v", c_base0)
                for r in range(4):
                    for kv in range(KVH):
                        ctx.dma(sync, "qg", st_all[:, r, kv : kv + 1], ag_st_out[4 + r, kv][None].rearrange("a p j -> p a j"))
            c_stall1 = C["qg"]

            @blk.vector
            def _(v):
                ctx.wait(v, "qg", c_stall1)
                for r in range(4):
                    v.scalar_tensor_tensor(st_base[:], st_all[:, r], pmask[:, 4 + r : 5 + r],
                                           st_base[:], Alu.mult, Alu.add)
                    v.drain()
                for ch in range(NSC - 1, 0, -1):
                    v.tensor_tensor(st_bf[:, :, ch, :], st_base[:], st_bf[:, :, ch - 1, :], Alu.add)
                v.drain()
                ins = v.tensor_copy(st_bf[:, :, 0, :], st_base[:])
                ins; ctx.dinc(v, "v")
            c_stbf = C["v"]

            if "vst" in DBG:
                @blk.gpsimd
                def _(g):
                    ctx.wait(g, "v", c_stbf)
                    ctx.dma(g, "qg", dbg["dbg_vst"], v_ext[:])
            if "stb" in DBG:
                @blk.gpsimd
                def _(g):
                    ctx.wait(g, "v", c_stbf)
                    ctx.dma(g, "qg", dbg["dbg_stb"], st_bf[:])
            t_A = {}; v_mask = {}; t_den = {}; v_den = {}; t_bc = {}; v_scl = {}; t_o = {}; v_oev = {}
            for k in range(NSC * KVH):
                ch = k // KVH
                kv = k % KVH
                pp = k % 2
                psAT = psB[:, pp * 512 : pp * 512 + 512]
                psDEN = psB[0:1, 1024 + pp * 512 : 1024 + pp * 512 + 512]
                psBC = psA[:, 1024 + pp * 512 : 1024 + pp * 512 + 512]
                psO = psA[:, pp * 512 : pp * 512 + 512]
                qf4 = xnfm[:, 4 * kv : 4 * kv + 4, ch * 128 : (ch + 1) * 128]

                @blk.tensor
                def _(te, k=k, ch=ch, kv=kv, pp=pp, psAT=psAT):
                    if k == 0:
                        ctx.wait(te, "qs", c_fm)
                        ctx.wait(te, "v", c_stbf)
                    if k >= 2:
                        ctx.wait(te, "v", v_oev[k - 2])
                    for i in range(4):
                        ins = te.matmul(psAT[:, i * 128 : (i + 1) * 128],
                                        h_flat[:, KO + kv * 1024 + ch * 128 : KO + kv * 1024 + (ch + 1) * 128],
                                        xnfm[:, 4 * kv + i, ch * 128 : (ch + 1) * 128],
                                        start=True, stop=True)
                    t_A[k] = ctx.inc(ins, "t")

                @blk.vector
                def _(v, k=k, pp=pp, psAT=psAT):
                    ctx.wait(v, "t", t_A[k])
                    ins = v.tensor_tensor(am_bf[:, pp, :], psAT,
                                          maskA[:, None, :].broadcast_to([128, 4, 128]), Alu.mult)
                    ins; v_mask[k] = ctx.dinc(v, "v")

                @blk.tensor
                def _(te, k=k, ch=ch, kv=kv, pp=pp, psDEN=psDEN, qf4=qf4):
                    ctx.wait(te, "v", v_mask[k])
                    te.matmul(psDEN, maskA[:, 127:128], am_bf[:, pp, :], start=True, stop=False)
                    ins = te.matmul(psDEN, st_bf[:, kv, ch, 128:129], qf4, start=False, stop=True)
                    t_den[k] = ctx.inc(ins, "t")

                @blk.vector
                def _(v, k=k, pp=pp, psDEN=psDEN):
                    ctx.wait(v, "t", t_den[k])
                    v.tensor_scalar(den_sb[0:1, pp, :], psDEN, EPS_AT, None, Alu.add)
                    v.drain()
                    v.reciprocal(den_sb[0:1, 2 + pp, :], den_sb[0:1, pp, :])
                    v.drain()
                    if "noden" in DBG:
                        ins = v.memset(den_b2[0:1, pp, :], 1.0)
                    else:
                        ins = v.tensor_copy(den_b2[0:1, pp, :], den_sb[0:1, 2 + pp, :])
                    ins; v_den[k] = ctx.dinc(v, "v")

                @blk.tensor
                def _(te, k=k, pp=pp, psBC=psBC):
                    ctx.wait(te, "v", v_den[k])
                    ins = te.matmul(psBC, maskA[0:1, 0:128], den_b2[0:1, pp, :], start=True, stop=True)
                    t_bc[k] = ctx.inc(ins, "t")

                @blk.vector
                def _(v, k=k, pp=pp, psBC=psBC, qf4=qf4):
                    ctx.wait(v, "t", t_bc[k])
                    v.tensor_tensor(am2_bf[:, pp, :], am_bf[:, pp, :], psBC, Alu.mult)
                    ins = v.tensor_tensor(qs_bf[:, pp, :], qf4, psBC, Alu.mult)
                    ins; v_scl[k] = ctx.dinc(v, "v")

                @blk.tensor
                def _(te, k=k, ch=ch, kv=kv, pp=pp, psO=psO):
                    ctx.wait(te, "v", v_scl[k])
                    te.matmul(psO, v_ext[:, ch, kv, 0:128], am2_bf[:, pp, :], start=True, stop=False)
                    ins = te.matmul(psO, st_bf[:, kv, ch, 0:128], qs_bf[:, pp, :], start=False, stop=True)
                    t_o[k] = ctx.inc(ins, "t")

                @blk.vector
                def _(v, k=k, ch=ch, kv=kv, pp=pp, psO=psO):
                    ctx.wait(v, "t", t_o[k])
                    dest = h_flat[:, 4 * kv * 1024 : (4 * kv + 4) * 1024].rearrange(
                        "p (h s) -> p h s", h=4)[:, :, ch * 128 : (ch + 1) * 128]
                    ins = v.tensor_copy(dest, psO.rearrange("p (h s) -> p h s", h=4))
                    ins; v_oev[k] = ctx.dinc(v, "v")

            if "ofm" in DBG:
                @blk.gpsimd
                def _(g):
                    ctx.wait(g, "v", C["v"])
                    ctx.dma(g, "qg", dbg["dbg_ofm"], h_flat[:, 0:16384])

            wo_t = {}
            wo_evac = {}
            c_attn_end_v = C["v"]
            for dmq in range(4):
                for jc in range(H):
                    kwo = dmq * H + jc
                    wo_tile = h_flat[:, 16384 + (kwo % 2) * 512 : 16384 + (kwo % 2) * 512 + 512]

                    @blk.sync
                    def _(sync, dmq=dmq, jc=jc, kwo=kwo, wo_tile=wo_tile):
                        if kwo == 0:
                            ctx.wait(sync, "cc", cc_after["o"])
                            ctx.wait(sync, "v", c_attn_end_v)
                        elif kwo >= 2:
                            ctx.wait(sync, "t", wo_t[kwo - 2])
                        ctx.dma(sync, "wp0" if kwo % 2 == 0 else "wp1", wo_tile,
                                ag_o_out[jc // 2, jc % 2, dmq])
                    c_wwo = C["wp0" if kwo % 2 == 0 else "wp1"]

                    c_v0 = None
                    if jc == 0:
                        c_v0 = C["v"] if dmq == 0 else wo_evac[dmq - 1]

                    @blk.tensor
                    def _(te, dmq=dmq, jc=jc, kwo=kwo, c_wwo=c_wwo, c_v0=c_v0, wo_tile=wo_tile):
                        ctx.wait(te, "wp0" if kwo % 2 == 0 else "wp1", c_wwo)
                        if c_v0 is not None:
                            ctx.wait(te, "v", c_v0)
                        for sc in range(NSC):
                            ps = psA if sc < 4 else psB
                            ins = te.matmul(ps[:, (sc % 4) * 512 : (sc % 4) * 512 + 512],
                                            h_flat[:, jc * 1024 + sc * 128 : jc * 1024 + (sc + 1) * 128],
                                            wo_tile,
                                            start=(jc == 0), stop=(jc == H - 1))
                        wo_t[kwo] = ctx.inc(ins, "t")

                @blk.vector
                def _(v, dmq=dmq):
                    ctx.wait(v, "t", wo_t[dmq * H + H - 1])
                    for sc in range(NSC):
                        ps = psA if sc < 4 else psB
                        pv = ps[:, (sc % 4) * 512 : (sc % 4) * 512 + 512]
                        tgt = x_sb[:, sc, dmq * 512 : (dmq + 1) * 512]
                        ins = v.tensor_tensor(tgt, pv, tgt, Alu.add)
                    ins; ctx.dinc(v, "v")
                wo_evac[dmq] = C["v"]

                @blk.vector
                def _(v, dmq=dmq):
                    emit_ln_partial(v, dmq)
                    ctx.dinc(v, "v")

                if dmq == 2:
                    for b in (42, 41):
                        @blk.sync
                        def _(sync, b=b):
                            if b == 42:
                                ctx.wait(sync, "t", qkv_t[(ct_order[NCT - 1], NSC - 1)])
                                ctx.wait(sync, "v", c_stbf)
                            for base in (0, 2 * HC * OBLK):
                                reg = arena[:, base + (b % 2) * HC * OBLK : base + (b % 2 + 1) * HC * OBLK]
                                regv = reg.rearrange("p (hc w) -> p hc w", hc=HC)
                                for r in range(NCORES):
                                    src = ag_g_out[r, b] if base == 0 else u_out(r, b)
                                    ctx.dma(sync, "qw", regv[:, 2 * r : 2 * r + 2, :].rearrange("p j o -> p (j o)"),
                                            src.rearrange("p j o -> p (j o)"))
                        ffn2_pref[b] = ("qw", C["qw"])

        ffn2_pref = {}
        emit_attention()
        if "x2" in DBG:
            @blk.gpsimd
            def _(g):
                ctx.wait(g, "v", C["v"])
                for sc in range(NSC):
                    ctx.dma(g, "qg", dbg["dbg_x2"].rearrange("(sc p) d -> p sc d", p=128)[:, sc, :], x_sb[:, sc, :])

        c_xn3 = emit_ln(partial=True)
        emit_ffn(c_xn3, [cc_after[("u", i)] for i in range(4)], lambda: cc_after["d"],
                 final_store=True, prefetched=ffn2_pref)

        @blk.gpsimd
        def _(g):
            ctx.wait(g, "qg", C["qg"])

        @blk.sync
        def _(sync):
            ctx.wait(sync, "qg", C["qg"])

    return nc



def _host_constants(core):
    t = np.arange(core * S_CORE, (core + 1) * S_CORE, dtype=np.float64)
    inv = 1.0 / (10000.0 ** (np.arange(0, HD, 2, dtype=np.float64) / HD))
    fr = np.outer(t, inv)
    emb = np.concatenate([fr, fr], axis=-1)
    cos = np.cos(emb).astype(np.float32)
    sin = np.sin(emb).astype(np.float32)
    sinm = np.concatenate([-sin[:, :64], sin[:, 64:]], axis=-1).astype(np.float32)
    return cos, sinm


def kernel(L0_attn_x, L0_attn_Wq, L0_attn_Wk, L0_attn_Wv, L0_attn_Wo,
           L0_ffn_lut, L0_ffn_gate_idx, L0_ffn_up_idx, L0_ffn_down_idx):
    x = np.ascontiguousarray(np.asarray(L0_attn_x, dtype=np.float32).reshape(8192, DM))
    wq = np.ascontiguousarray(np.asarray(L0_attn_Wq, dtype=np.float32))
    wk = np.ascontiguousarray(np.asarray(L0_attn_Wk, dtype=np.float32))
    wv = np.ascontiguousarray(np.asarray(L0_attn_Wv, dtype=np.float32))
    wo = np.ascontiguousarray(np.asarray(L0_attn_Wo, dtype=np.float32))
    lut = np.ascontiguousarray(np.asarray(L0_ffn_lut, dtype=np.float32))
    gidx = np.ascontiguousarray(np.asarray(L0_ffn_gate_idx, dtype=np.int32))
    uidx = np.ascontiguousarray(np.asarray(L0_ffn_up_idx, dtype=np.int32))
    didx = np.ascontiguousarray(np.asarray(L0_ffn_down_idx, dtype=np.int32))
    maskA = np.triu(np.ones((128, 128), dtype=np.float32)).astype(ml_dtypes.bfloat16)
    didx_sh = []
    for c in range(NCORES):
        cols = []
        for s_ in range(6):
            ic = c + 8 * s_
            if ic < NOC:
                cols.append(didx[:, ic * 128 : (ic + 1) * 128])
            else:
                cols.append(np.zeros((DM, 128), dtype=np.int32))
        didx_sh.append(np.ascontiguousarray(np.concatenate(cols, axis=1)))

    nc = build_program()
    in_maps = []
    for c in range(NCORES):
        cos, sinm = _host_constants(c)
        cos = cos[:, :64].astype(ml_dtypes.bfloat16)
        sinm = sinm.astype(ml_dtypes.bfloat16)
        pm = np.zeros(NCORES, dtype=np.float32)
        pm[:c] = 1.0
        in_maps.append({
            "x_in": x[c * S_CORE : (c + 1) * S_CORE],
            "gidx": np.ascontiguousarray(gidx[:, c * HSH : (c + 1) * HSH]),
            "uidx": np.ascontiguousarray(uidx[:, c * HSH : (c + 1) * HSH]),
            "didx": didx_sh[c],
            "wq_s": np.ascontiguousarray(wq[:, c * HSH : (c + 1) * HSH]),
            "wk_s": np.ascontiguousarray(wk[:, c * HSH : (c + 1) * HSH]),
            "wv_s": np.ascontiguousarray(wv[:, c * HSH : (c + 1) * HSH]),
            "wo_s": np.ascontiguousarray(wo[:, c * HSH : (c + 1) * HSH]),
            "lut_in": lut,
            "cos_in": cos,
            "sinm_in": sinm,
            "maskA_in": maskA,
            "pmask_in": pm,
        })
    res = run_bass_kernel_spmd(nc, in_maps, core_ids=list(range(NCORES)))
    if DBG:
        kernel._dbg = res.results
    outp = np.concatenate([np.asarray(res.results[c]["out"]) for c in range(NCORES)], axis=0)
    return outp.reshape(1, 8192, DM)


# version 41
# speedup vs baseline: 1.6291x; 1.6291x over previous
import os
import numpy as np
import ml_dtypes
import concourse.bass as bass
import concourse.mybir as mybir
from concourse.bass_utils import run_bass_kernel_spmd

F32 = mybir.dt.float32
BF16 = mybir.dt.bfloat16
I32 = mybir.dt.int32
Alu = mybir.AluOpType
Act = mybir.ActivationFunctionType
AxX = mybir.AxisListType.X

NCORES = 8
S_CORE = 1024
NSC = 8
DM = 2048
HC = 16
INTER = 5504
NOC = 43
H = 16
KVH = 4
HD = 128
DQ = H * HD
DKV = KVH * HD
QKV = DQ + 2 * DKV
HSH = DM // NCORES
NICD = 6
EPS_LN = 1e-5
EPS_AT = 1e-6
OBLK = 128
DBG = os.environ.get("MKDBG", "")


class Ctx:
    def __init__(self, nc):
        self.nc = nc
        self.sem = {}
        self.cnt = {}

    def mk_sem(self, stack, *names):
        for n in names:
            self.sem[n] = stack.enter_context(self.nc.semaphore(n))
            self.cnt[n] = 0

    def dma(self, eng, sname, out, in_, **kw):
        eng.dma_start(out=out, in_=in_, **kw).then_inc(self.sem[sname], 16)
        self.cnt[sname] += 16
        return self.cnt[sname]

    def dmat(self, eng, sname, out, in_):
        eng.dma_start_transpose(out, in_).then_inc(self.sem[sname], 16)
        self.cnt[sname] += 16
        return self.cnt[sname]

    def inc(self, ins, sname, n=1):
        ins.then_inc(self.sem[sname], n)
        self.cnt[sname] += n
        return self.cnt[sname]

    def dinc(self, eng, sname, n=1):
        eng.drain().then_inc(self.sem[sname], n)
        self.cnt[sname] += n
        return self.cnt[sname]

    def wait(self, eng, sname, val=None):
        if val is None:
            val = self.cnt[sname]
        if val > 0:
            eng.wait_ge(self.sem[sname], val)


def build_program():
    nc = bass.Bass("TRN2", target_bir_lowering=False)
    import contextlib

    x_in = nc.dram_tensor("x_in", [S_CORE, DM], F32, kind="ExternalInput").ap()
    gidx = nc.dram_tensor("gidx", [INTER, HSH], I32, kind="ExternalInput").ap()
    uidx = nc.dram_tensor("uidx", [INTER, HSH], I32, kind="ExternalInput").ap()
    didx = nc.dram_tensor("didx", [DM, NICD * 128], I32, kind="ExternalInput").ap()
    wq_s = nc.dram_tensor("wq_s", [DQ, HSH], F32, kind="ExternalInput").ap()
    wk_s = nc.dram_tensor("wk_s", [DKV, HSH], F32, kind="ExternalInput").ap()
    wv_s = nc.dram_tensor("wv_s", [DKV, HSH], F32, kind="ExternalInput").ap()
    wo_s = nc.dram_tensor("wo_s", [DM, HSH], F32, kind="ExternalInput").ap()
    lut_in = nc.dram_tensor("lut_in", [16], F32, kind="ExternalInput").ap()
    cos_in = nc.dram_tensor("cos_in", [S_CORE, 64], BF16, kind="ExternalInput").ap()
    sinm_in = nc.dram_tensor("sinm_in", [S_CORE, HD], BF16, kind="ExternalInput").ap()
    maskA_in = nc.dram_tensor("maskA_in", [128, 128], BF16, kind="ExternalInput").ap()
    pmask_in = nc.dram_tensor("pmask_in", [NCORES], F32, kind="ExternalInput").ap()

    out = nc.dram_tensor("out", [S_CORE, DM], F32, kind="ExternalOutput").ap()

    gidx_bf = nc.dram_tensor("gidx_bf", [INTER, HSH], BF16).ap()
    uidx_bf = nc.dram_tensor("uidx_bf", [INTER, HSH], BF16).ap()
    didx_bf = nc.dram_tensor("didx_bf", [DM, NICD * 128], BF16).ap()
    wq_bf = nc.dram_tensor("wq_bf", [DQ, HSH], BF16).ap()
    wk_bf = nc.dram_tensor("wk_bf", [DKV, HSH], BF16).ap()
    wv_bf = nc.dram_tensor("wv_bf", [DKV, HSH], BF16).ap()
    wo_bf = nc.dram_tensor("wo_bf", [DM, HSH], BF16).ap()
    ag_g_in = nc.dram_tensor("ag_g_in", [NOC, 128, 2, 128], BF16).ap()
    CH_B = [0, 11, 22, 33, 43]
    ag_g_out = nc.dram_tensor("ag_g_out", [NCORES, NOC, 128, 2, 128], BF16, addr_space="Shared").ap()
    ag_u_in = nc.dram_tensor("ag_u_in", [NOC, 128, 2, 128], BF16).ap()
    ag_u_out_hi = nc.dram_tensor("ag_u_out_hi", [NCORES, NOC - 22, 128, 2, 128], BF16, addr_space="Shared").ap()
    ag_u_out_lo = nc.dram_tensor("ag_u_out_lo", [NCORES, 22, 128, 2, 128], BF16, addr_space="Shared").ap()
    ag_d_in = nc.dram_tensor("ag_d_in", [NICD, 4, 128, 512], BF16).ap()
    ag_d_out = nc.dram_tensor("ag_d_out", [NCORES, NICD, 4, 128, 512], BF16, addr_space="Shared").ap()
    ag_qkv_in = nc.dram_tensor("ag_qkv_in", [12, 128, 2, 256], BF16).ap()
    ag_qkv_out = nc.dram_tensor("ag_qkv_out", [NCORES, 12, 128, 2, 256], BF16, addr_space="Shared").ap()
    ag_o_in = nc.dram_tensor("ag_o_in", [2, 4, 128, 512], BF16).ap()
    ag_o_out = nc.dram_tensor("ag_o_out", [NCORES, 2, 4, 128, 512], BF16, addr_space="Shared").ap()
    ag_st_in = nc.dram_tensor("ag_st_in", [KVH, 128, 132], BF16).ap()
    ag_st_out = nc.dram_tensor("ag_st_out", [NCORES, KVH, 128, 132], BF16, addr_space="Shared").ap()
    fl_in = nc.dram_tensor("fl_in", [16], BF16).ap()
    fl_outs = {nm: nc.dram_tensor(f"fl_{nm}", [NCORES, 16], BF16).ap() for nm in ("gu", "gu2", "d", "qo", "st")}
    xn_hbm = nc.dram_tensor("xn_hbm", [S_CORE, DM], BF16).ap()
    qk_hbm = nc.dram_tensor("qk_hbm", [S_CORE, DQ + DKV], BF16).ap()

    dbg = {}
    if "x1" in DBG:
        dbg["dbg_x1"] = nc.dram_tensor("dbg_x1", [S_CORE, DM], F32, kind="ExternalOutput").ap()

    if "x2" in DBG:
        dbg["dbg_x2"] = nc.dram_tensor("dbg_x2", [S_CORE, DM], F32, kind="ExternalOutput").ap()
    if "wgb" in DBG:
        dbg["dbg_wgb"] = nc.dram_tensor("dbg_wgb", [NCORES, NOC, 128, 2, 128], F32, kind="ExternalOutput").ap()
    if "wdb" in DBG:
        dbg["dbg_wdb"] = nc.dram_tensor("dbg_wdb", [NCORES, NICD, 4, 128, 512], F32, kind="ExternalOutput").ap()
    if "qk" in DBG:
        dbg["dbg_qk"] = nc.dram_tensor("dbg_qk", [S_CORE, DQ + DKV], F32, kind="ExternalOutput").ap()
    if "vst" in DBG:
        dbg["dbg_vst"] = nc.dram_tensor("dbg_vst", [128, NSC, KVH, 132], F32, kind="ExternalOutput").ap()
    if "stb" in DBG:
        dbg["dbg_stb"] = nc.dram_tensor("dbg_stb", [128, KVH, 8, 132], F32, kind="ExternalOutput").ap()
    if "ofm" in DBG:
        dbg["dbg_ofm"] = nc.dram_tensor("dbg_ofm", [128, 16384], F32, kind="ExternalOutput").ap()

    if "xn1" in DBG:
        dbg["dbg_xn1"] = nc.dram_tensor("dbg_xn1", [128, HC * S_CORE], F32, kind="ExternalOutput").ap()
    if "xs" in DBG:
        dbg["dbg_xs"] = nc.dram_tensor("dbg_xs", [128, NSC, DM], F32, kind="ExternalOutput").ap()
    if "stat" in DBG:
        dbg["dbg_stat"] = nc.dram_tensor("dbg_stat", [128, 8, 8], F32, kind="ExternalOutput").ap()
    if "hfm" in DBG:
        dbg["dbg_hfm"] = nc.dram_tensor("dbg_hfm", [128, NOC * S_CORE], F32, kind="ExternalOutput").ap()

    ctx = Ctx(nc)
    with contextlib.ExitStack() as stack:
        ent = stack.enter_context
        x_sb = ent(nc.sbuf_tensor("x_sb", [128, NSC, DM], F32))
        xnfm = ent(nc.sbuf_tensor("xnfm", [128, HC, S_CORE], BF16))
        h_flat = ent(nc.sbuf_tensor("h_flat", [128, NOC * S_CORE], BF16))
        arena = ent(nc.sbuf_tensor("arena", [128, 8192], BF16))
        tmp_a = ent(nc.sbuf_tensor("tmp_a", [128, 1024], F32))
        tmp_bf = ent(nc.sbuf_tensor("tmp_bf", [128, 2048], BF16))
        gw_i = h_flat[:, 0:INTER]
        gw_t = h_flat[:, INTER : 2 * INTER]
        gw_a = h_flat[:, 2 * INTER : 3 * INTER]
        st_bf = h_flat[:, 20480 : 20480 + 4224].rearrange("p (kv ch j) -> p kv ch j", kv=KVH, ch=8)
        kf_tm = h_flat[:, 24704 : 24704 + 4096].rearrange("p (sc kv d) -> p sc kv d", sc=NSC, kv=KVH)
        v_ext = h_flat[:, 28800 : 28800 + 4224].rearrange("p (sc kv j) -> p sc kv j", sc=NSC, kv=KVH)
        st_all = arena[:, 0 : 2112].rearrange("p (r kv j) -> p r kv j", r=4, kv=KVH)
        st_base = arena[:, 2112 : 2112 + 1056].bitcast(F32).rearrange("p (kv j) -> p kv j", kv=KVH)
        cos_sb = h_flat[:, 33024 : 33024 + 512].rearrange("p (sc d) -> p sc d", sc=NSC)
        sinm_sb = h_flat[:, 33536 : 33536 + 1024].rearrange("p (sc d) -> p sc d", sc=NSC)
        den_sb = h_flat[0:1, 34560 : 34560 + 4096].bitcast(F32).rearrange("p (a b) -> p a b", a=4)
        den_b2 = h_flat[0:1, 38656 : 38656 + 1024].rearrange("p (a b) -> p a b", a=2)
        am_bf = h_flat[:, 39680 : 39680 + 1024].rearrange("p (a b) -> p a b", a=2)
        qs_bf = h_flat[:, 40704 : 40704 + 1024].rearrange("p (a b) -> p a b", a=2)
        am2_bf = h_flat[:, 41728 : 41728 + 1024].rearrange("p (a b) -> p a b", a=2)
        lut_sb = ent(nc.sbuf_tensor("lut_sb", [128, 16], F32))
        maskA = ent(nc.sbuf_tensor("maskA", [128, 128], BF16))
        pmask = ent(nc.sbuf_tensor("pmask", [128, NCORES], F32))
        stat = ent(nc.sbuf_tensor("stat", [128, 8, 16], F32))


        psA = ent(nc.psum_tensor("psA", [128, 2048], F32))
        psB = ent(nc.psum_tensor("psB", [128, 2048], F32))

        ctx.mk_sem(stack, "qs", "qa", "qb", "qg", "qx", "qw", "qy", "qz", "qp", "wp0", "wp1", "v", "t", "act", "cc")
        C = ctx.cnt

        blk = ent(nc.Block())

        @blk.sync
        def _(sync):
            ctx.dma(sync, "qw", lut_sb[:], lut_in[None, :].broadcast_to([128, 16]))
            ctx.dma(sync, "qw", maskA[:], maskA_in)
            ctx.dma(sync, "qw", pmask[:], pmask_in[None, :].broadcast_to([128, NCORES]))
        c_consts = C["qw"]

        @blk.sync
        def _(sync):
            for sc in range(NSC):
                ctx.dma(sync, "qx", x_sb[:, sc, :],
                        x_in.rearrange("(sc p) d -> p sc d", p=128)[:, sc, :])

        @blk.gpsimd
        def _(g):
            ctx.dma(g, "qw", gidx_bf, gidx)
            ctx.dma(g, "qw", uidx_bf, uidx)
            ctx.dma(g, "qw", didx_bf, didx)
            ctx.dma(g, "qw", wq_bf, wq_s)
            ctx.dma(g, "qw", wk_bf, wk_s)
            ctx.dma(g, "qw", wv_bf, wv_s)
            ctx.dma(g, "qw", wo_bf, wo_s)
        c_x_loaded = C["qx"]
        c_casts = C["qw"]

        @blk.vector
        def _(v):
            v.memset(tmp_a[:], 0.0); ctx.dinc(v, "v")

        def emit_ln_partial(v, dmq):
            v.drain()
            for sc in range(NSC):
                tgt = x_sb[:, sc, dmq * 512 : (dmq + 1) * 512]
                v.reduce_sum(stat[:, sc, 8 + dmq : 9 + dmq], tgt, axis=AxX)
                v.tensor_tensor(tmp_bf[:, 0:512], tgt, tgt, Alu.mult)
                v.reduce_sum(stat[:, sc, 12 + dmq : 13 + dmq], tmp_bf[:, 0:512], axis=AxX)

        def emit_ln_stats(qg_wait=0, partial=False):
            @blk.vector
            def _(v):
                if qg_wait:
                    ctx.wait(v, "qx", qg_wait)
                if partial:
                    v.drain()
                    v.reduce_sum(stat[:, :, 0:1], stat[:, :, 8:12], axis=AxX)
                    v.reduce_sum(stat[:, :, 1:2], stat[:, :, 12:16], axis=AxX)
                else:
                    for sc in range(NSC):
                        v.reduce_sum(stat[:, sc, 0:1], x_sb[:, sc, :], axis=AxX)
                        v.tensor_tensor(tmp_bf[:], x_sb[:, sc, :], x_sb[:, sc, :], Alu.mult)
                        v.reduce_sum(stat[:, sc, 1:2], tmp_bf[:], axis=AxX)
                v.drain()
                v.tensor_scalar(stat[:, :, 2:3], stat[:, :, 0:1], 1.0 / DM, None, Alu.mult)
                v.tensor_scalar(stat[:, :, 3:4], stat[:, :, 1:2], 1.0 / DM, None, Alu.mult)
                v.drain()
                v.tensor_tensor(stat[:, :, 4:5], stat[:, :, 2:3], stat[:, :, 2:3], Alu.mult)
                v.drain()
                v.tensor_tensor(stat[:, :, 5:6], stat[:, :, 3:4], stat[:, :, 4:5], Alu.subtract)
                v.drain()
                ins = v.tensor_scalar(stat[:, :, 5:6], stat[:, :, 5:6], EPS_LN, None, Alu.add)
                ins; ctx.dinc(v, "v")
            c_var = C["v"]

            @blk.scalar
            def _(se):
                ctx.wait(se, "v", c_var)
                se.activation(stat[:, :, 6:7], stat[:, :, 5:6], Act.Sqrt); ctx.dinc(se, "act")
            c_sqrt = C["act"]

            @blk.vector
            def _(v):
                ctx.wait(v, "act", c_sqrt)
                v.reciprocal(stat[:, :, 7:8], stat[:, :, 6:7])
                ctx.dinc(v, "v")

            @blk.vector
            def _(v):
                ctx.wait(v, "v", C["v"])
                v.drain()
                ins = v.scalar_tensor_tensor(stat[:, :, 4:5], stat[:, :, 2:3], -1.0,
                                             stat[:, :, 7:8], Alu.mult, Alu.mult)
                ins; ctx.dinc(v, "v")
            return C["v"]

        def emit_ln_apply(c_nb):
            tb2 = tmp_a[:].bitcast(BF16)
            store_after = {}
            for sc in range(NSC):
                @blk.scalar
                def _(se, sc=sc):
                    sname = "qa" if sc % 2 == 0 else "qb"
                    if sc == 0:
                        ctx.wait(se, "v", c_nb)
                    if sc >= 2:
                        ctx.wait(se, sname, store_after[sc - 2])
                    buf = tmp_bf[:] if sc % 2 == 0 else tb2
                    se.activation(buf, x_sb[:, sc, :], Act.Identity,
                                  bias=stat[:, sc, 4:5], scale=stat[:, sc, 7:8])
                    se.drain()
                    store_after[sc] = ctx.dma(
                        se, sname, xn_hbm.rearrange("(sc p) d -> p sc d", p=128)[:, sc, :], buf)
            return store_after

        def emit_ln_xbars(store_after, xbar_sem="qs", full=False):
            c_t_entry = C["t"]
            if full:
                @blk.sync
                def _(sync):
                    ctx.wait(sync, "qa", store_after[NSC - 2])
                    ctx.wait(sync, "qb", store_after[NSC - 1])
                    ctx.wait(sync, "t", c_t_entry)
                    for hcx in range(HC):
                        ctx.dmat(sync, xbar_sem, xnfm[:, hcx, :],
                                 xn_hbm[:, hcx * 128 : (hcx + 1) * 128])
                return (xbar_sem, [C[xbar_sem]] * 4)
            pcnt = []
            for p in range(NSC // 2):
                @blk.sync
                def _(sync, p=p):
                    ctx.wait(sync, "qa", store_after[2 * p])
                    ctx.wait(sync, "qb", store_after[2 * p + 1])
                    if p == 0:
                        ctx.wait(sync, "t", c_t_entry)
                    for hcx in range(HC):
                        ctx.dmat(sync, xbar_sem, xnfm[:, hcx, p * 256 : (p + 1) * 256],
                                 xn_hbm[p * 256 : (p + 1) * 256, hcx * 128 : (hcx + 1) * 128])
                pcnt.append(C[xbar_sem])
            return (xbar_sem, pcnt)

        def emit_ln(qg_wait=0, partial=False):
            return emit_ln_xbars(emit_ln_apply(emit_ln_stats(qg_wait, partial=partial)))

        cc_after = {}
        c_gu_store = {}
        for src_bf, ag_t, wps in [(gidx_bf, ag_g_in, "wp0"), (uidx_bf, ag_u_in, "wp1")]:
            for j in range(2):
                @blk.sync
                def _(sync, src_bf=src_bf, j=j):
                    ctx.wait(sync, "qw", c_casts)
                    ctx.wait(sync, "v", C["v"])
                    ctx.dmat(sync, "qs", gw_i, src_bf[:, j * 128 : (j + 1) * 128])

                @blk.vector
                def _(v):
                    ctx.wait(v, "qs", C["qs"])
                    ctx.wait(v, "wp0", C["wp0"])
                    ctx.wait(v, "wp1", C["wp1"])
                    v.tensor_scalar(gw_a, gw_i, 0.0, lut_sb[:, 0:1], Alu.is_equal, Alu.mult)
                    for t in range(1, 16):
                        v.tensor_scalar(gw_t, gw_i, float(t), lut_sb[:, t : t + 1], Alu.is_equal, Alu.mult)
                        ins = v.tensor_tensor(gw_a, gw_a, gw_t, Alu.add)
                    ins; ctx.dinc(v, "v")

                @blk.scalar
                def _(se, ag_t=ag_t, j=j, wps=wps):
                    ctx.wait(se, "v", C["v"])
                    for c0 in range(0, NOC, 2):
                        cw = min(2, NOC - c0)
                        ctx.dma(se, wps, ag_t[c0 : c0 + cw, :, j, :].rearrange("oc p o -> p oc o"),
                                gw_a.rearrange("p (oc o) -> p oc o", o=128)[:, c0 : c0 + cw, :])
                c_gu_store[(wps, j)] = C[wps]
            if wps == "wp0":
                ln1_sa = emit_ln_apply(emit_ln_stats(qg_wait=c_x_loaded))
        c_gu_gather = C["v"]

        RG = [list(range(NCORES))]
        def u_out(r, b):
            return ag_u_out_hi[r, b - 22] if b >= 22 else ag_u_out_lo[r, b]
        def emit_flag(g, nm):
            ctx.wait(g, "cc", C["cc"])
            ins = g.collective_compute("AllGather", Alu.bypass, replica_groups=RG,
                                       ins=[fl_in], outs=[fl_outs[nm]])
            return ctx.inc(ins, "cc")

        @blk.gpsimd
        def _(g):
            ctx.wait(g, "wp0", c_gu_store[("wp0", 1)])
            ins = g.collective_compute("AllGather", Alu.bypass, replica_groups=RG,
                                       ins=[ag_g_in], outs=[ag_g_out])
            ctx.inc(ins, "cc")
            ctx.wait(g, "wp1", c_gu_store[("wp1", 1)])
            ins = g.collective_compute("AllGather", Alu.bypass, replica_groups=RG,
                                       ins=[ag_u_in[22:NOC]], outs=[ag_u_out_hi])
            ctx.inc(ins, "cc")
            c_fl = emit_flag(g, "gu")
            cc_after[("u", 3)] = cc_after[("u", 2)] = c_fl
            ins = g.collective_compute("AllGather", Alu.bypass, replica_groups=RG,
                                       ins=[ag_u_in[0:22]], outs=[ag_u_out_lo])
            ctx.inc(ins, "cc")
            c_fl = emit_flag(g, "gu2")
            cc_after[("u", 1)] = cc_after[("u", 0)] = c_fl
        c_xn1 = emit_ln_xbars(ln1_sa, xbar_sem="qw", full=True)

        c_dg = {}
        c_dst = {}
        def emit_dprep(kd):
            @blk.sync
            def _(sync, kd=kd):
                ctx.wait(sync, "v", c_gu_gather if kd == 0 else c_dg[kd - 1])
                ctx.dmat(sync, "qs", gw_i[:, 0:DM], didx_bf[:, kd * 128 : (kd + 1) * 128])

            @blk.vector
            def _(v, kd=kd):
                ctx.wait(v, "qs", C["qs"])
                if kd == 0:
                    ctx.wait(v, "wp1", c_gu_store[("wp1", 1)])
                else:
                    ctx.wait(v, "qg", c_dst[kd - 1])
                v.tensor_scalar(gw_a[:, 0:DM], gw_i[:, 0:DM], 0.0, lut_sb[:, 0:1], Alu.is_equal, Alu.mult)
                for t in range(1, 16):
                    v.tensor_scalar(gw_t[:, 0:DM], gw_i[:, 0:DM], float(t), lut_sb[:, t : t + 1], Alu.is_equal, Alu.mult)
                    ins = v.tensor_tensor(gw_a[:, 0:DM], gw_a[:, 0:DM], gw_t[:, 0:DM], Alu.add)
                ins; c_dg[kd] = ctx.dinc(v, "v")

            @blk.gpsimd
            def _(g, kd=kd):
                ctx.wait(g, "v", c_dg[kd])
                for q in range(4):
                    ctx.dma(g, "qg", ag_d_in[kd, q][None], gw_a[:, q * 512 : (q + 1) * 512][:, None, :])
            c_dst[kd] = C["qg"]
        D_OF_BLOCK = {42: 0, 40: 1, 38: 2, 36: 3, 34: 4, 32: 5}

        PH = {}
        qkv_pref = {}

        def emit_qkv_prefetch():
            for pos in (0, 1):
                @blk.sync
                def _(sync, pos=pos):
                    if pos == 0:
                        ctx.wait(sync, "cc", cc_after["qkv"])
                        ctx.wait(sync, "t", PH["gate_t"])
                    reg = arena[:, (pos % 2) * 4096 : (pos % 2) * 4096 + 4096].rearrange("p (hc w) -> p hc w", hc=HC)
                    for r in range(NCORES):
                        ctx.dma(sync, "qw", reg[:, 2 * r : 2 * r + 2, :].rearrange("p j o -> p (j o)"),
                                ag_qkv_out[r, 8 + pos].rearrange("p j o -> p (j o)"))
                qkv_pref[pos] = ("qw", C["qw"])

        def emit_attn_prep(c_gate_v, c_gate_t):
            PH["gate_t"] = c_gate_t
            PH["gate_v"] = c_gate_v

            @blk.gpsimd
            def _(g):
                ctx.wait(g, "qg", C["qg"])
                ins = g.collective_compute("AllGather", Alu.bypass, replica_groups=RG,
                                           ins=[ag_d_in], outs=[ag_d_out])
                ctx.inc(ins, "cc")
                cc_after["d"] = emit_flag(g, "d")

        def emit_attn_stage(j):
            qst, wst = 1024, 4096

            @blk.sync
            def _(sync, j=j):
                if j == 0:
                    ctx.wait(sync, "v", PH["gate_v"])
                else:
                    ctx.wait(sync, "qp", C["qp"])
                ctx.dmat(sync, "qp", arena[:, qst : qst + DQ], wq_bf[:, j * 128 : (j + 1) * 128])
                ctx.dmat(sync, "qp", arena[:, qst + DQ : qst + DQ + DKV], wk_bf[:, j * 128 : (j + 1) * 128])
                ctx.dmat(sync, "qp", arena[:, qst + DQ + DKV : qst + QKV], wv_bf[:, j * 128 : (j + 1) * 128])
                ctx.dmat(sync, "qp", arena[:, wst : wst + DM], wo_bf[:, j * 128 : (j + 1) * 128])

            @blk.scalar
            def _(se, j=j):
                ctx.wait(se, "qp", C["qp"])
                for c0 in range(0, 12, 2):
                    ctx.dma(se, "qp", ag_qkv_in[c0 : c0 + 2, :, j, :].rearrange("ct p o -> p ct o"),
                            arena[:, qst : qst + QKV].rearrange("p (ct o) -> p ct o", o=256)[:, c0 : c0 + 2, :])
                for q in range(4):
                    ctx.dma(se, "qp", ag_o_in[j, q][None], arena[:, wst + q * 512 : wst + (q + 1) * 512][:, None, :])

        def emit_qo_ag():
            c_qp = C["qp"]

            @blk.gpsimd
            def _(g):
                ctx.wait(g, "qp", c_qp)
                for ain, aout in [(ag_qkv_in, ag_qkv_out), (ag_o_in, ag_o_out)]:
                    ins = g.collective_compute("AllGather", Alu.bypass, replica_groups=RG,
                                               ins=[ain], outs=[aout])
                    ctx.inc(ins, "cc")
                c_fl = emit_flag(g, "qo")
                cc_after["qkv"] = cc_after["o"] = c_fl

        def ffn1_down_hook(dmq):
            if dmq == 0:
                emit_attn_stage(0)
            elif dmq == 1:
                emit_attn_stage(1)
            elif dmq == 2:
                emit_qo_ag()
            elif dmq == 3:
                emit_qkv_prefetch()


        def emit_ffn(c_xnfm_ready, cc_gu, get_cc_d, final_store=False,
                     block_hook=None, post_gate_hook=None, post_down_hook=None,
                     prefetched=None):
            xn_sem_n, xn_cnts = c_xnfm_ready
            t_entry = C["t"]
            nblk = (INTER + OBLK - 1) // OBLK
            CH_BOUNDS = CH_B

            def slab(base, b, hcx, width):
                off = base + (b % 2) * HC * OBLK + hcx * OBLK
                return arena[:, off : off + width]

            act_free = {0: C["act"], 1: C["act"]}
            v_free = {0: C["v"], 1: C["v"]}
            t_after_grp = {}
            t_after_blk = {}
            c_wg_blk = {}
            kseq = 0
            bs = list(range(nblk - 1, -1, -1))
            for idx, b in enumerate(bs):
                o0 = b * OBLK
                ow = min(OBLK, INTER - o0)
                noc_b = (ow + 127) // 128

                if prefetched is not None and b in prefetched:
                    c_wg_blk[b] = prefetched[b]
                else:
                    @blk.sync
                    def _(sync, b=b, idx=idx, o0=o0, ow=ow):
                        ch = 0 if b < 11 else (1 if b < 22 else (2 if b < 33 else 3))
                        if idx == 0:
                            ctx.wait(sync, "t", t_entry)
                        elif idx >= 2:
                            ctx.wait(sync, "t", t_after_blk[bs[idx - 2]])
                        if b == CH_BOUNDS[ch + 1] - 1:
                            ctx.wait(sync, "cc", cc_gu[ch])
                        wps = "wp0" if b % 2 == 0 else "wp1"
                        for mat, base in ((None, 0), (None, 2 * HC * OBLK)):
                            reg = arena[:, base + (b % 2) * HC * OBLK : base + (b % 2 + 1) * HC * OBLK]
                            regv = reg.rearrange("p (hc w) -> p hc w", hc=HC)
                            for r in range(NCORES):
                                src = ag_g_out[r, b] if base == 0 else u_out(r, b)
                                ctx.dma(sync, wps, regv[:, 2 * r : 2 * r + 2, :].rearrange("p j o -> p (j o)"),
                                        src.rearrange("p j o -> p (j o)"))
                    c_wg_blk[b] = ("wp0" if b % 2 == 0 else "wp1", C["wp0" if b % 2 == 0 else "wp1"])

                @blk.tensor
                def _(te, b=b, ow=ow, noc_b=noc_b, kseq=kseq):
                    te.wait_ge(ctx.sem[c_wg_blk[b][0]], c_wg_blk[b][1])
                    for oj in range(noc_b):
                        k = kseq + oj
                        ocw = min(128, ow - oj * 128)
                        pp = k % 2
                        pg = psA[:, pp * 1024 : pp * 1024 + 1024]
                        pu = psB[:, pp * 1024 : pp * 1024 + 1024]
                        ctx.wait(te, "act", act_free[pp])
                        ctx.wait(te, "v", v_free[pp])
                        for sh in range(2):
                            te.wait_ge(ctx.sem[xn_sem_n], xn_cnts[2 * sh + 1])
                            for hcx in range(HC):
                                te.matmul(pg[:ocw, sh * 512 : (sh + 1) * 512],
                                          slab(0, b, hcx, ow)[:, oj * 128 : oj * 128 + ocw],
                                          xnfm[:, hcx, sh * 512 : (sh + 1) * 512],
                                          start=(hcx == 0), stop=(hcx == HC - 1))
                            for hcx in range(HC):
                                ins = te.matmul(pu[:ocw, sh * 512 : (sh + 1) * 512],
                                          slab(2 * HC * OBLK, b, hcx, ow)[:, oj * 128 : oj * 128 + ocw],
                                          xnfm[:, hcx, sh * 512 : (sh + 1) * 512],
                                          start=(hcx == 0), stop=(hcx == HC - 1))
                        t_after_grp[k] = ctx.inc(ins, "t")
                    t_after_blk[b] = C["t"]

                for oj in range(noc_b):
                    k = kseq + oj
                    oc = (o0 + oj * 128) // 128
                    ocw = min(128, ow - oj * 128)
                    pp = k % 2
                    pg = psA[:, pp * 1024 : pp * 1024 + 1024]
                    pu = psB[:, pp * 1024 : pp * 1024 + 1024]

                    @blk.scalar
                    def _(se, k=k, pg=pg, ocw=ocw, pp=pp):
                        ctx.wait(se, "t", t_after_grp[k])
                        if k >= 2:
                            ctx.wait(se, "v", v_free[pp])
                        ins = se.activation(tmp_bf[:ocw, pp * 1024 : pp * 1024 + 1024], pg[:ocw, :], Act.Silu)
                        ins; act_free[pp] = ctx.dinc(se, "act")

                    @blk.vector
                    def _(v, k=k, pu=pu, oc=oc, ocw=ocw, pp=pp):
                        ctx.wait(v, "act", act_free[pp])
                        ins = v.tensor_tensor(h_flat[:ocw, oc * 1024 : (oc + 1) * 1024], tmp_bf[:ocw, pp * 1024 : pp * 1024 + 1024], pu[:ocw, :], Alu.mult)
                        ins; v_free[pp] = ctx.dinc(v, "v")
                kseq += noc_b
                if block_hook is not None:
                    block_hook(b)
            c_gate_end_t = C["t"]
            c_gate_end_act = C["act"]
            c_gate_end_v = C["v"]
            if post_gate_hook is not None:
                post_gate_hook(c_gate_end_v, c_gate_end_t)

            c_pp1_free = v_free[1]
            wd_t_after = {}
            v_after_dmq = {}
            v_evac = {}
            fs_dma_after = {}
            for dmq in range(4):
                for ic in range(NOC):
                    k = dmq * NOC + ic
                    dtile = tmp_bf[:, 1024 + (k % 2) * 512 : 1024 + (k % 2) * 512 + 512]

                    @blk.sync
                    def _(sync, dmq=dmq, ic=ic, k=k, dtile=dtile):
                        if k == 0:
                            ctx.wait(sync, "cc", get_cc_d())
                            ctx.wait(sync, "v", c_pp1_free)
                        elif k >= 2:
                            ctx.wait(sync, "t", wd_t_after[k - 2])
                        ctx.dma(sync, "wp0" if k % 2 == 0 else "wp1", dtile,
                                ag_d_out[ic % NCORES, ic // NCORES, dmq])
                    c_wd = C["wp0" if k % 2 == 0 else "wp1"]

                    c_v0 = None
                    if ic == 0:
                        c_v0 = C["v"] if dmq == 0 else v_evac[dmq - 1]

                    @blk.tensor
                    def _(te, dmq=dmq, ic=ic, k=k, c_wd=c_wd, c_v0=c_v0, dtile=dtile):
                        ctx.wait(te, "wp0" if k % 2 == 0 else "wp1", c_wd)
                        if c_v0 is not None:
                            ctx.wait(te, "v", c_v0)
                        if ic == 0 and dmq == 0:
                            ctx.wait(te, "act", C["act"])
                        for sc in range(NSC):
                            ps = psA if sc < 4 else psB
                            ins = te.matmul(ps[:, (sc % 4) * 512 : (sc % 4) * 512 + 512],
                                            h_flat[:, ic * 1024 + sc * 128 : ic * 1024 + (sc + 1) * 128],
                                            dtile,
                                            start=(ic == 0), stop=(ic == NOC - 1))
                        wd_t_after[k] = ctx.inc(ins, "t")

                if not final_store:
                    @blk.vector
                    def _(v, dmq=dmq):
                        ctx.wait(v, "t", wd_t_after[dmq * NOC + NOC - 1])
                        for sc in range(NSC):
                            ps = psA if sc < 4 else psB
                            pv = ps[:, (sc % 4) * 512 : (sc % 4) * 512 + 512]
                            tgt = x_sb[:, sc, dmq * 512 : (dmq + 1) * 512]
                            ins = v.scalar_tensor_tensor(tgt, pv, 0.5, tgt, Alu.mult, Alu.add)
                        ins; ctx.dinc(v, "v")
                    v_evac[dmq] = C["v"]

                    @blk.vector
                    def _(v, dmq=dmq):
                        emit_ln_partial(v, dmq)
                        ctx.dinc(v, "v")
                else:
                    @blk.vector
                    def _(v, dmq=dmq):
                        ctx.wait(v, "t", wd_t_after[dmq * NOC + NOC - 1])
                        for sc in range(NSC):
                            ps = psA if sc < 4 else psB
                            pv = ps[:, (sc % 4) * 512 : (sc % 4) * 512 + 512]
                            tgt = x_sb[:, sc, dmq * 512 : (dmq + 1) * 512]
                            ins = v.scalar_tensor_tensor(tgt, pv, 0.5, tgt, Alu.mult, Alu.add)
                        ins; ctx.dinc(v, "v")

                    v_evac[dmq] = C["v"]

                    @blk.scalar
                    def _(se, dmq=dmq):
                        ctx.wait(se, "v", C["v"])
                        fs_dma_after[dmq] = ctx.dma(
                            se, "qg",
                            out.rearrange("(sc p) d -> p sc d", p=128)[:, :, dmq * 512 : (dmq + 1) * 512],
                            x_sb[:, :, dmq * 512 : (dmq + 1) * 512])
                v_after_dmq[dmq] = C["v"]
                if post_down_hook is not None:
                    post_down_hook(dmq)
            return {"v_dmq": v_after_dmq, "gate_t": c_gate_end_t}

        if "wgb" in DBG:
            @blk.gpsimd
            def _(g):
                ctx.wait(g, "cc", cc_after[("u", 0)])
                ctx.dma(g, "qg", dbg["dbg_wgb"], ag_g_out)
        if "wdb" in DBG:
            @blk.gpsimd
            def _(g):
                ctx.wait(g, "cc", cc_after["d"])
                ctx.dma(g, "qg", dbg["dbg_wdb"], ag_d_out)
        if "xs" in DBG:
            @blk.gpsimd
            def _(g):
                ctx.wait(g, "qx", c_x_loaded)
                ctx.dma(g, "qg", dbg["dbg_xs"], x_sb[:])
        if "xn1" in DBG:
            @blk.gpsimd
            def _(g):
                ctx.wait(g, "qw", c_xn1[1][3])
                ctx.dma(g, "qg", dbg["dbg_xn1"], xnfm[:])
        if "stat" in DBG:
            @blk.gpsimd
            def _(g):
                ctx.wait(g, "qw", c_xn1[1][3])
                ctx.dma(g, "qg", dbg["dbg_stat"], stat[:])
        ffn1_info = emit_ffn(c_xn1, [cc_after[("u", i)] for i in range(4)], lambda: cc_after["d"],
                             block_hook=lambda b: emit_dprep(D_OF_BLOCK[b]) if b in D_OF_BLOCK else None,
                             post_gate_hook=emit_attn_prep,
                             post_down_hook=ffn1_down_hook)
        if "hfm" in DBG:
            @blk.gpsimd
            def _(g):
                ctx.wait(g, "v", C["v"])
                ctx.dma(g, "qg", dbg["dbg_hfm"], h_flat[:])

        if "x1" in DBG:
            @blk.gpsimd
            def _(g):
                ctx.wait(g, "v", C["v"])
                for sc in range(NSC):
                    ctx.dma(g, "qg", dbg["dbg_x1"].rearrange("(sc p) d -> p sc d", p=128)[:, sc, :], x_sb[:, sc, :])


        def emit_attention():
            c_xn2 = emit_ln(partial=True)
            KO = 16384

            NCT = 12
            CTW = 256
            t_entry_q = C["t"]
            qkv_t = {}
            v_evac = {}
            act_aft = {}
            store_q = {}
            c_w_ct = {}
            ct_order = [8, 9, 10, 11, 0, 1, 2, 3, 4, 5, 6, 7]
            st_t = {}
            st_v = {}
            for pos, ct in enumerate(ct_order):
                if pos < 2:
                    c_w_ct[pos] = qkv_pref[pos]
                else:
                    @blk.sync
                    def _(sync, ct=ct, pos=pos):
                        ctx.wait(sync, "t", qkv_t[(ct_order[pos - 2], NSC - 1)])
                        reg = arena[:, (pos % 2) * 4096 : (pos % 2) * 4096 + 4096].rearrange("p (hc w) -> p hc w", hc=HC)
                        wps = "wp0" if pos % 2 == 0 else "wp1"
                        for r in range(NCORES):
                            ctx.dma(sync, wps, reg[:, 2 * r : 2 * r + 2, :].rearrange("p j o -> p (j o)"),
                                    ag_qkv_out[r, ct].rearrange("p j o -> p (j o)"))
                    c_w_ct[pos] = ("wp0" if pos % 2 == 0 else "wp1", C["wp0" if pos % 2 == 0 else "wp1"])

                if pos == 0:
                    @blk.gpsimd
                    def _(g):
                        ctx.wait(g, "t", t_entry_q)
                        ctx.dma(g, "qz", cos_sb[:], cos_in.rearrange("(sc p) d -> p sc d", p=128))
                        ctx.dma(g, "qz", sinm_sb[:], sinm_in.rearrange("(sc p) d -> p sc d", p=128))
                    c_cos2 = C["qz"]

                if ct == 10:
                    @blk.vector
                    def _(v):
                        ctx.wait(v, "t", qkv_t[(9, NSC - 1)])
                        v.memset(v_ext[:, :, :, 128:129], 1.0)
                        v.memset(v_ext[:, :, :, 129:132], 0.0); ctx.dinc(v, "v")
                for sc in range(NSC):
                    kq = pos * NSC + sc
                    pp = kq % 2
                    psq = psA[:, pp * 512 : pp * 512 + CTW]

                    @blk.tensor
                    def _(te, ct=ct, sc=sc, pos=pos, kq=kq, pp=pp, psq=psq):
                        te.wait_ge(ctx.sem[c_w_ct[pos][0]], c_w_ct[pos][1])
                        te.wait_ge(ctx.sem[c_xn2[0]], c_xn2[1][sc // 2])
                        if kq == 0:
                            ctx.wait(te, "v", C["v"])
                        if kq >= 2:
                            ctx.wait(te, "v", v_evac[kq - 2])
                        for hcx in range(HC):
                            ins = te.matmul(psq, xnfm[:, hcx, sc * 128 : (sc + 1) * 128],
                                            arena[:, (pos % 2) * 4096 + hcx * CTW : (pos % 2) * 4096 + hcx * CTW + CTW],
                                            start=(hcx == 0), stop=(hcx == HC - 1))
                        qkv_t[(ct, sc)] = ctx.inc(ins, "t")

                    if ct < 10:
                        @blk.vector
                        def _(v, ct=ct, sc=sc, kq=kq, pp=pp, psq=psq):
                            ctx.wait(v, "t", qkv_t[(ct, sc)])
                            if kq == 0:
                                ctx.wait(v, "qz", c_cos2)
                            if kq >= 2:
                                ctx.wait(v, "act", act_aft.get(kq - 2, 0))
                            if ct < 8 and (kq - 1) in store_q:
                                ctx.wait(v, "qa", store_q[kq - 1])
                            psv = psq.rearrange("p (h two d) -> p h two d", h=2, two=2)
                            psw = psv[:, :, ::-1, :]
                            slotA = tmp_a[:, pp * 512 : pp * 512 + CTW]
                            slotB = tmp_a[:, pp * 512 + 256 : pp * 512 + 256 + CTW]
                            cosb = cos_sb[:, sc, None, None, :].broadcast_to([128, 2, 2, 64])
                            sinb = sinm_sb[:, sc, :].rearrange("p (two d) -> p two d", two=2)[
                                :, None, :, :].broadcast_to([128, 2, 2, 64])
                            v.tensor_tensor(slotA, psq, cosb, Alu.mult)
                            v.tensor_tensor(slotB, psw, sinb, Alu.mult)
                            v.drain()
                            v.tensor_tensor(slotA, slotA, slotB, Alu.add)
                            v.drain()
                            ins = v.tensor_single_scalar(slotB, slotA, 0.0, Alu.min)
                            ins; ctx.dinc(v, "v")
                        c_rope = C["v"]

                        @blk.scalar
                        def _(se, kq=kq, pp=pp, c_rope=c_rope):
                            ctx.wait(se, "v", c_rope)
                            se.activation(tmp_bf[:, pp * 1024 + 512 : pp * 1024 + 512 + CTW],
                                          tmp_a[:, pp * 512 + 256 : pp * 512 + 256 + CTW], Act.Exp)
                            act_aft[kq] = ctx.dinc(se, "act")

                        @blk.vector
                        def _(v, ct=ct, sc=sc, kq=kq, pp=pp):
                            ctx.wait(v, "act", act_aft[kq])
                            if ct < 8:
                                dest = tmp_bf[:, pp * 1024 : pp * 1024 + CTW]
                            else:
                                dest = kf_tm[:, sc, 2 * (ct - 8) : 2 * (ct - 8) + 2, :]
                            v.tensor_single_scalar(tmp_a[:, pp * 512 + 256 : pp * 512 + 256 + CTW],
                                                   tmp_a[:, pp * 512 : pp * 512 + CTW], 0.0, Alu.max)
                            v.drain()
                            ins = v.tensor_tensor(dest, tmp_bf[:, pp * 1024 + 512 : pp * 1024 + 512 + CTW],
                                                  tmp_a[:, pp * 512 + 256 : pp * 512 + 256 + CTW], Alu.add)
                            ins; v_evac[kq] = ctx.dinc(v, "v")

                        if ct < 8:
                            @blk.scalar
                            def _(se, ct=ct, sc=sc, kq=kq, pp=pp):
                                ctx.wait(se, "v", v_evac[kq])
                                store_q[kq] = ctx.dma(
                                    se, "qa",
                                    qk_hbm.rearrange("(sc p) c -> p sc c", p=128)[:, sc, ct * CTW : (ct + 1) * CTW],
                                    tmp_bf[:, pp * 1024 : pp * 1024 + CTW])
                    else:
                        @blk.vector
                        def _(v, ct=ct, sc=sc, kq=kq, psq=psq):
                            ctx.wait(v, "t", qkv_t[(ct, sc)])
                            ins = v.tensor_copy(v_ext[:, sc, 2 * (ct - 10) : 2 * (ct - 10) + 2, 0:128],
                                                psq.rearrange("p (kv d) -> p kv d", kv=2))
                            ins; v_evac[kq] = ctx.dinc(v, "v")

                if 4 <= pos <= 7:
                    for chs in (2 * (pos - 4), 2 * (pos - 4) + 1):
                        @blk.tensor
                        def _(te, ch=chs):
                            if ch == 0:
                                ctx.wait(te, "v", C["v"])
                            else:
                                ctx.wait(te, "v", st_v[ch - 1])
                            for kv in range(KVH):
                                ins = te.matmul(psB[:, kv * 512 : kv * 512 + 132],
                                                kf_tm[:, ch, kv, :],
                                                v_ext[:, ch, kv, :],
                                                start=(ch == 0), stop=(ch == NSC - 1))
                            st_t[ch] = ctx.inc(ins, "t")

                        @blk.vector
                        def _(v, ch=chs):
                            ctx.wait(v, "t", st_t[ch])
                            ins = v.tensor_copy(st_bf[:, :, ch, :],
                                                psB[:].rearrange("p (kv r) -> p kv r", kv=4)[:, :, 0:132])
                            ins; st_v[ch] = ctx.dinc(v, "v")
                if pos == 7:
                    @blk.gpsimd
                    def _(g):
                        ctx.wait(g, "v", st_v[NSC - 1])
                        for kv in range(KVH):
                            ctx.dma(g, "qy", ag_st_in[kv][None].rearrange("a p j -> p a j"),
                                    st_bf[:, kv : kv + 1, NSC - 1, :])
                        ctx.wait(g, "qy", C["qy"])
                        ins = g.collective_compute("AllGather", Alu.bypass, replica_groups=RG,
                                                   ins=[ag_st_in], outs=[ag_st_out])
                        ctx.inc(ins, "cc")
                        emit_flag(g, "st")
                    c_cc_st = C["cc"]

            @blk.scalar
            def _(se):
                ctx.wait(se, "v", v_evac[2 * NSC - 1])
                ctx.dma(se, "qa", qk_hbm.rearrange("(sc p) c -> p sc c", p=128)[:, :, DQ : DQ + DKV],
                        kf_tm[:].rearrange("p sc kv d -> p sc (kv d)"))
            c_kf_store = C["qa"]

            @blk.sync
            def _(sync):
                ctx.wait(sync, "qa", max(c_kf_store, max(store_q.values())))
                ctx.wait(sync, "t", qkv_t[(ct_order[NCT - 1], NSC - 1)])
                for hq in range(H):
                    ctx.dmat(sync, "qs", xnfm[:, hq, :], qk_hbm[:, hq * 128 : (hq + 1) * 128])
                for kv in range(KVH):
                    ctx.dmat(sync, "qs", h_flat[:, KO + kv * 1024 : KO + (kv + 1) * 1024],
                             qk_hbm[:, DQ + kv * 128 : DQ + (kv + 1) * 128])
            c_fm = C["qs"]

            if "qk" in DBG:
                @blk.gpsimd
                def _(g):
                    ctx.wait(g, "qs", c_fm)
                    ctx.dma(g, "qg", dbg["dbg_qk"], qk_hbm)
            @blk.sync
            def _(sync):
                ctx.wait(sync, "cc", c_cc_st)
                ctx.wait(sync, "t", qkv_t[(ct_order[10], NSC - 1)])
                for r in range(4):
                    for kv in range(KVH):
                        ctx.dma(sync, "qg", st_all[:, r, kv : kv + 1], ag_st_out[r, kv][None].rearrange("a p j -> p a j"))
            c_stall0 = C["qg"]

            @blk.vector
            def _(v):
                ctx.wait(v, "qg", c_stall0)
                v.memset(st_base[:], 0.0)
                v.drain()
                for r in range(4):
                    v.scalar_tensor_tensor(st_base[:], st_all[:, r], pmask[:, r : r + 1],
                                           st_base[:], Alu.mult, Alu.add)
                    v.drain()
                ctx.dinc(v, "v")
            c_base0 = C["v"]

            @blk.sync
            def _(sync):
                ctx.wait(sync, "v", c_base0)
                for r in range(4):
                    for kv in range(KVH):
                        ctx.dma(sync, "qg", st_all[:, r, kv : kv + 1], ag_st_out[4 + r, kv][None].rearrange("a p j -> p a j"))
            c_stall1 = C["qg"]

            @blk.vector
            def _(v):
                ctx.wait(v, "qg", c_stall1)
                for r in range(4):
                    v.scalar_tensor_tensor(st_base[:], st_all[:, r], pmask[:, 4 + r : 5 + r],
                                           st_base[:], Alu.mult, Alu.add)
                    v.drain()
                for ch in range(NSC - 1, 0, -1):
                    v.tensor_tensor(st_bf[:, :, ch, :], st_base[:], st_bf[:, :, ch - 1, :], Alu.add)
                v.drain()
                ins = v.tensor_copy(st_bf[:, :, 0, :], st_base[:])
                ins; ctx.dinc(v, "v")
            c_stbf = C["v"]

            if "vst" in DBG:
                @blk.gpsimd
                def _(g):
                    ctx.wait(g, "v", c_stbf)
                    ctx.dma(g, "qg", dbg["dbg_vst"], v_ext[:])
            if "stb" in DBG:
                @blk.gpsimd
                def _(g):
                    ctx.wait(g, "v", c_stbf)
                    ctx.dma(g, "qg", dbg["dbg_stb"], st_bf[:])
            t_A = {}; v_mask = {}; t_den = {}; v_den = {}; t_bc = {}; v_scl = {}; t_o = {}; v_oev = {}
            for k in range(NSC * KVH):
                ch = k // KVH
                kv = k % KVH
                pp = k % 2
                psAT = psB[:, pp * 512 : pp * 512 + 512]
                psDEN = psB[0:1, 1024 + pp * 512 : 1024 + pp * 512 + 512]
                psBC = psA[:, 1024 + pp * 512 : 1024 + pp * 512 + 512]
                psO = psA[:, pp * 512 : pp * 512 + 512]
                qf4 = xnfm[:, 4 * kv : 4 * kv + 4, ch * 128 : (ch + 1) * 128]

                @blk.tensor
                def _(te, k=k, ch=ch, kv=kv, pp=pp, psAT=psAT):
                    if k == 0:
                        ctx.wait(te, "qs", c_fm)
                        ctx.wait(te, "v", c_stbf)
                    if k >= 2:
                        ctx.wait(te, "v", v_oev[k - 2])
                    for i in range(4):
                        ins = te.matmul(psAT[:, i * 128 : (i + 1) * 128],
                                        h_flat[:, KO + kv * 1024 + ch * 128 : KO + kv * 1024 + (ch + 1) * 128],
                                        xnfm[:, 4 * kv + i, ch * 128 : (ch + 1) * 128],
                                        start=True, stop=True)
                    t_A[k] = ctx.inc(ins, "t")

                @blk.vector
                def _(v, k=k, pp=pp, psAT=psAT):
                    ctx.wait(v, "t", t_A[k])
                    ins = v.tensor_tensor(am_bf[:, pp, :], psAT,
                                          maskA[:, None, :].broadcast_to([128, 4, 128]), Alu.mult)
                    ins; v_mask[k] = ctx.dinc(v, "v")

                @blk.tensor
                def _(te, k=k, ch=ch, kv=kv, pp=pp, psDEN=psDEN, qf4=qf4):
                    ctx.wait(te, "v", v_mask[k])
                    te.matmul(psDEN, maskA[:, 127:128], am_bf[:, pp, :], start=True, stop=False)
                    ins = te.matmul(psDEN, st_bf[:, kv, ch, 128:129], qf4, start=False, stop=True)
                    t_den[k] = ctx.inc(ins, "t")

                @blk.vector
                def _(v, k=k, pp=pp, psDEN=psDEN):
                    ctx.wait(v, "t", t_den[k])
                    v.tensor_scalar(den_sb[0:1, pp, :], psDEN, EPS_AT, None, Alu.add)
                    v.drain()
                    v.reciprocal(den_sb[0:1, 2 + pp, :], den_sb[0:1, pp, :])
                    v.drain()
                    if "noden" in DBG:
                        ins = v.memset(den_b2[0:1, pp, :], 1.0)
                    else:
                        ins = v.tensor_copy(den_b2[0:1, pp, :], den_sb[0:1, 2 + pp, :])
                    ins; v_den[k] = ctx.dinc(v, "v")

                @blk.tensor
                def _(te, k=k, pp=pp, psBC=psBC):
                    ctx.wait(te, "v", v_den[k])
                    ins = te.matmul(psBC, maskA[0:1, 0:128], den_b2[0:1, pp, :], start=True, stop=True)
                    t_bc[k] = ctx.inc(ins, "t")

                @blk.vector
                def _(v, k=k, pp=pp, psBC=psBC, qf4=qf4):
                    ctx.wait(v, "t", t_bc[k])
                    v.tensor_tensor(am2_bf[:, pp, :], am_bf[:, pp, :], psBC, Alu.mult)
                    ins = v.tensor_tensor(qs_bf[:, pp, :], qf4, psBC, Alu.mult)
                    ins; v_scl[k] = ctx.dinc(v, "v")

                @blk.tensor
                def _(te, k=k, ch=ch, kv=kv, pp=pp, psO=psO):
                    ctx.wait(te, "v", v_scl[k])
                    te.matmul(psO, v_ext[:, ch, kv, 0:128], am2_bf[:, pp, :], start=True, stop=False)
                    ins = te.matmul(psO, st_bf[:, kv, ch, 0:128], qs_bf[:, pp, :], start=False, stop=True)
                    t_o[k] = ctx.inc(ins, "t")

                @blk.vector
                def _(v, k=k, ch=ch, kv=kv, pp=pp, psO=psO):
                    ctx.wait(v, "t", t_o[k])
                    dest = h_flat[:, 4 * kv * 1024 : (4 * kv + 4) * 1024].rearrange(
                        "p (h s) -> p h s", h=4)[:, :, ch * 128 : (ch + 1) * 128]
                    ins = v.tensor_copy(dest, psO.rearrange("p (h s) -> p h s", h=4))
                    ins; v_oev[k] = ctx.dinc(v, "v")

            if "ofm" in DBG:
                @blk.gpsimd
                def _(g):
                    ctx.wait(g, "v", C["v"])
                    ctx.dma(g, "qg", dbg["dbg_ofm"], h_flat[:, 0:16384])

            wo_t = {}
            wo_evac = {}
            c_attn_end_v = C["v"]
            for dmq in range(4):
                for jc in range(H):
                    kwo = dmq * H + jc
                    wo_tile = h_flat[:, 16384 + (kwo % 2) * 512 : 16384 + (kwo % 2) * 512 + 512]

                    @blk.sync
                    def _(sync, dmq=dmq, jc=jc, kwo=kwo, wo_tile=wo_tile):
                        if kwo == 0:
                            ctx.wait(sync, "cc", cc_after["o"])
                            ctx.wait(sync, "v", c_attn_end_v)
                        elif kwo >= 2:
                            ctx.wait(sync, "t", wo_t[kwo - 2])
                        ctx.dma(sync, "wp0" if kwo % 2 == 0 else "wp1", wo_tile,
                                ag_o_out[jc // 2, jc % 2, dmq])
                    c_wwo = C["wp0" if kwo % 2 == 0 else "wp1"]

                    c_v0 = None
                    if jc == 0:
                        c_v0 = C["v"] if dmq == 0 else wo_evac[dmq - 1]

                    @blk.tensor
                    def _(te, dmq=dmq, jc=jc, kwo=kwo, c_wwo=c_wwo, c_v0=c_v0, wo_tile=wo_tile):
                        ctx.wait(te, "wp0" if kwo % 2 == 0 else "wp1", c_wwo)
                        if c_v0 is not None:
                            ctx.wait(te, "v", c_v0)
                        for sc in range(NSC):
                            ps = psA if sc < 4 else psB
                            ins = te.matmul(ps[:, (sc % 4) * 512 : (sc % 4) * 512 + 512],
                                            h_flat[:, jc * 1024 + sc * 128 : jc * 1024 + (sc + 1) * 128],
                                            wo_tile,
                                            start=(jc == 0), stop=(jc == H - 1))
                        wo_t[kwo] = ctx.inc(ins, "t")

                @blk.vector
                def _(v, dmq=dmq):
                    ctx.wait(v, "t", wo_t[dmq * H + H - 1])
                    for sc in range(NSC):
                        ps = psA if sc < 4 else psB
                        pv = ps[:, (sc % 4) * 512 : (sc % 4) * 512 + 512]
                        tgt = x_sb[:, sc, dmq * 512 : (dmq + 1) * 512]
                        ins = v.tensor_tensor(tgt, pv, tgt, Alu.add)
                    ins; ctx.dinc(v, "v")
                wo_evac[dmq] = C["v"]

                @blk.vector
                def _(v, dmq=dmq):
                    emit_ln_partial(v, dmq)
                    ctx.dinc(v, "v")

                if dmq == 2:
                    for b in (42, 41):
                        @blk.sync
                        def _(sync, b=b):
                            if b == 42:
                                ctx.wait(sync, "t", qkv_t[(ct_order[NCT - 1], NSC - 1)])
                                ctx.wait(sync, "v", c_stbf)
                            for base in (0, 2 * HC * OBLK):
                                reg = arena[:, base + (b % 2) * HC * OBLK : base + (b % 2 + 1) * HC * OBLK]
                                regv = reg.rearrange("p (hc w) -> p hc w", hc=HC)
                                for r in range(NCORES):
                                    src = ag_g_out[r, b] if base == 0 else u_out(r, b)
                                    ctx.dma(sync, "qw", regv[:, 2 * r : 2 * r + 2, :].rearrange("p j o -> p (j o)"),
                                            src.rearrange("p j o -> p (j o)"))
                        ffn2_pref[b] = ("qw", C["qw"])

        ffn2_pref = {}
        emit_attention()
        if "x2" in DBG:
            @blk.gpsimd
            def _(g):
                ctx.wait(g, "v", C["v"])
                for sc in range(NSC):
                    ctx.dma(g, "qg", dbg["dbg_x2"].rearrange("(sc p) d -> p sc d", p=128)[:, sc, :], x_sb[:, sc, :])

        c_xn3 = emit_ln(partial=True)
        emit_ffn(c_xn3, [cc_after[("u", i)] for i in range(4)], lambda: cc_after["d"],
                 final_store=True, prefetched=ffn2_pref)

        @blk.gpsimd
        def _(g):
            ctx.wait(g, "qg", C["qg"])

        @blk.sync
        def _(sync):
            ctx.wait(sync, "qg", C["qg"])

    return nc



def _host_constants(core):
    t = np.arange(core * S_CORE, (core + 1) * S_CORE, dtype=np.float64)
    inv = 1.0 / (10000.0 ** (np.arange(0, HD, 2, dtype=np.float64) / HD))
    fr = np.outer(t, inv)
    emb = np.concatenate([fr, fr], axis=-1)
    cos = np.cos(emb).astype(np.float32)
    sin = np.sin(emb).astype(np.float32)
    sinm = np.concatenate([-sin[:, :64], sin[:, 64:]], axis=-1).astype(np.float32)
    return cos, sinm


def kernel(L0_attn_x, L0_attn_Wq, L0_attn_Wk, L0_attn_Wv, L0_attn_Wo,
           L0_ffn_lut, L0_ffn_gate_idx, L0_ffn_up_idx, L0_ffn_down_idx):
    x = np.ascontiguousarray(np.asarray(L0_attn_x, dtype=np.float32).reshape(8192, DM))
    wq = np.ascontiguousarray(np.asarray(L0_attn_Wq, dtype=np.float32))
    wk = np.ascontiguousarray(np.asarray(L0_attn_Wk, dtype=np.float32))
    wv = np.ascontiguousarray(np.asarray(L0_attn_Wv, dtype=np.float32))
    wo = np.ascontiguousarray(np.asarray(L0_attn_Wo, dtype=np.float32))
    lut = np.ascontiguousarray(np.asarray(L0_ffn_lut, dtype=np.float32))
    gidx = np.ascontiguousarray(np.asarray(L0_ffn_gate_idx, dtype=np.int32))
    uidx = np.ascontiguousarray(np.asarray(L0_ffn_up_idx, dtype=np.int32))
    didx = np.ascontiguousarray(np.asarray(L0_ffn_down_idx, dtype=np.int32))
    maskA = np.triu(np.ones((128, 128), dtype=np.float32)).astype(ml_dtypes.bfloat16)
    didx_sh = []
    for c in range(NCORES):
        cols = []
        for s_ in range(6):
            ic = c + 8 * s_
            if ic < NOC:
                cols.append(didx[:, ic * 128 : (ic + 1) * 128])
            else:
                cols.append(np.zeros((DM, 128), dtype=np.int32))
        didx_sh.append(np.ascontiguousarray(np.concatenate(cols, axis=1)))

    nc = build_program()
    in_maps = []
    for c in range(NCORES):
        cos, sinm = _host_constants(c)
        cos = cos[:, :64].astype(ml_dtypes.bfloat16)
        sinm = sinm.astype(ml_dtypes.bfloat16)
        pm = np.zeros(NCORES, dtype=np.float32)
        pm[:c] = 1.0
        in_maps.append({
            "x_in": x[c * S_CORE : (c + 1) * S_CORE],
            "gidx": np.ascontiguousarray(gidx[:, c * HSH : (c + 1) * HSH]),
            "uidx": np.ascontiguousarray(uidx[:, c * HSH : (c + 1) * HSH]),
            "didx": didx_sh[c],
            "wq_s": np.ascontiguousarray(wq[:, c * HSH : (c + 1) * HSH]),
            "wk_s": np.ascontiguousarray(wk[:, c * HSH : (c + 1) * HSH]),
            "wv_s": np.ascontiguousarray(wv[:, c * HSH : (c + 1) * HSH]),
            "wo_s": np.ascontiguousarray(wo[:, c * HSH : (c + 1) * HSH]),
            "lut_in": lut,
            "cos_in": cos,
            "sinm_in": sinm,
            "maskA_in": maskA,
            "pmask_in": pm,
        })
    res = run_bass_kernel_spmd(nc, in_maps, core_ids=list(range(NCORES)))
    if DBG:
        kernel._dbg = res.results
    outp = np.concatenate([np.asarray(res.results[c]["out"]) for c in range(NCORES)], axis=0)
    return outp.reshape(1, 8192, DM)
